# version 07
# baseline (speedup 1.0000x reference)
import numpy as np
import ml_dtypes
from contextlib import ExitStack

import concourse.bass as bass
import concourse.mybir as mybir
from concourse.bass_utils import run_bass_kernel_spmd

F32 = mybir.dt.float32
BF16 = mybir.dt.bfloat16
I32 = mybir.dt.int32
AF = mybir.ActivationFunctionType
ALU = mybir.AluOpType
AX = mybir.AxisListType

D = 4096
CC = 32
BTOK = 8192
TOWN = 2048
DH = 1024
NDC = 8
FQ = 3072
INTER = 11008
NIC = 86
EPS = 1e-5
ASCL = 0.03125

NT = 64
TB = 512
NTB = 16
NTB_OWN = 4
NQT = 16
NKB = 16
NKC = 64
NTBF = 4
TG = 1024
NTG = 2
NOB = 8
NTS = 8

PHASE_LIMIT = 6
DBG = False


class Sem:
    __slots__ = ("h", "n")

    def __init__(self, nc, ctx, name):
        self.h = ctx.enter_context(nc.semaphore(name))
        self.n = 0

    def inc(self, instr, k=1):
        instr.then_inc(self.h, k)
        self.n += k
        return self.n


def build_nc(dbg=False, phase_limit=PHASE_LIMIT):
    nc = bass.Bass("TRN2", target_bir_lowering=False)
    sk = "ExternalOutput" if dbg else "Internal"

    x_own = nc.dram_tensor("x_own", [TOWN, D], F32, kind="ExternalInput")
    x_rest = nc.dram_tensor("x_rest", [BTOK - TOWN, D], F32, kind="ExternalInput")
    wqkv = nc.dram_tensor("wqkv", [D, FQ], F32, kind="ExternalInput")
    bqkv_pp = nc.dram_tensor("bqkv_pp", [128, 24], F32, kind="ExternalInput")
    bqkv_row = nc.dram_tensor("bqkv_row", [1, FQ], F32, kind="ExternalInput")
    wo = nc.dram_tensor("wo", [DH, D], F32, kind="ExternalInput")
    bo_row = nc.dram_tensor("bo_row", [1, D], F32, kind="ExternalInput")
    wgT = nc.dram_tensor("wgT", [D, INTER], I32, kind="ExternalInput")
    wuT = nc.dram_tensor("wuT", [D, INTER], I32, kind="ExternalInput")
    wdT = nc.dram_tensor("wdT", [INTER, D], I32, kind="ExternalInput")
    sg_pp = nc.dram_tensor("sg_pp", [128, NIC], F32, kind="ExternalInput")
    su_pp = nc.dram_tensor("su_pp", [128, NIC], F32, kind="ExternalInput")
    sd_row = nc.dram_tensor("sd_row", [1, D], F32, kind="ExternalInput")
    ones_in = nc.dram_tensor("ones_in", [1, 128], BF16, kind="ExternalInput")
    ident_in = nc.dram_tensor("ident_in", [128, 128], BF16, kind="ExternalInput")

    out = nc.dram_tensor("out", [TOWN, D], F32, kind="ExternalOutput")

    xn_sc = nc.dram_tensor("xn_sc", [BTOK, D], BF16, kind=sk)
    kT_sc = nc.dram_tensor("kT_sc", [DH, BTOK], BF16, kind=sk)
    v_sc = nc.dram_tensor("v_sc", [BTOK, DH], BF16, kind=sk)
    qT_sc = nc.dram_tensor("qT_sc", [DH, TOWN], BF16, kind=sk)
    pa_sc = nc.dram_tensor("pa_sc", [TOWN, D], F32, kind=sk)
    h_sc = nc.dram_tensor("h_sc", [INTER, TOWN], BF16, kind=sk)

    ctx = ExitStack()
    with ctx:
        bqkv_sb = ctx.enter_context(nc.sbuf_tensor("bqkv_sb", [128, 24], F32))
        bq_s = ctx.enter_context(nc.sbuf_tensor("bq_s", [128, 8], F32))
        ones_bf = ctx.enter_context(nc.sbuf_tensor("ones_bf", [1, 128], BF16))
        ident_bf = ctx.enter_context(nc.sbuf_tensor("ident_bf", [128, 128], BF16))
        sg_sb = ctx.enter_context(nc.sbuf_tensor("sg_sb", [128, NIC], F32))
        su_sb = ctx.enter_context(nc.sbuf_tensor("su_sb", [128, NIC], F32))

        pre = Sem(nc, ctx, "pre")
        with nc.Block() as blk:
            @blk.sync
            def _(sync):
                pre.inc(sync.dma_start(out=bqkv_sb[:], in_=bqkv_pp[:, :]), 16)
                pre.inc(sync.dma_start(out=sg_sb[:], in_=sg_pp[:, :]), 16)
                pre.inc(sync.dma_start(out=su_sb[:], in_=su_pp[:, :]), 16)
                pre.inc(sync.dma_start(out=ones_bf[:], in_=ones_in[:, :]), 16)
                pre.inc(sync.dma_start(out=ident_bf[:], in_=ident_in[:, :]), 16)

            @blk.vector
            def _(vector):
                vector.wait_ge(pre.h, 80)
                vector.tensor_scalar_mul(bq_s[:], bqkv_sb[:, 0:8], float(ASCL))

        if phase_limit >= 1:
            with ExitStack() as pctx:
                xt = [pctx.enter_context(nc.sbuf_tensor(f"xt{i}", [128, D], F32)) for i in range(2)]
                xnb = [pctx.enter_context(nc.sbuf_tensor(f"xnb{i}", [128, D], BF16)) for i in range(2)]
                dump = pctx.enter_context(nc.sbuf_tensor("dump", [128, D], F32))
                st1 = [pctx.enter_context(nc.sbuf_tensor(f"st1_{i}", [128, 1], F32)) for i in range(2)]
                st2 = [pctx.enter_context(nc.sbuf_tensor(f"st2_{i}", [128, 1], F32)) for i in range(2)]
                mean = [pctx.enter_context(nc.sbuf_tensor(f"mean{i}", [128, 1], F32)) for i in range(2)]
                msq = [pctx.enter_context(nc.sbuf_tensor(f"msq{i}", [128, 1], F32)) for i in range(2)]
                var = [pctx.enter_context(nc.sbuf_tensor(f"var{i}", [128, 1], F32)) for i in range(2)]
                v4 = [pctx.enter_context(nc.sbuf_tensor(f"v4_{i}", [128, 1], F32)) for i in range(2)]
                lnv = [pctx.enter_context(nc.sbuf_tensor(f"lnv{i}", [128, 1], F32)) for i in range(2)]
                rstd = [pctx.enter_context(nc.sbuf_tensor(f"rstd{i}", [128, 1], F32)) for i in range(2)]
                lx = [Sem(nc, pctx, f"lx{i}") for i in range(2)]
                xnw = [Sem(nc, pctx, f"xnw{i}") for i in range(2)]
                vap = Sem(nc, pctx, "vap")
                vred = Sem(nc, pctx, "vred")
                ssq = Sem(nc, pctx, "ssq")
                svar = Sem(nc, pctx, "svar")
                srst = Sem(nc, pctx, "srst")
                sself = Sem(nc, pctx, "sself")

                def xsrc(g):
                    if g < 16:
                        return x_own[g * 128:(g + 1) * 128, :]
                    return x_rest[(g - 16) * 128:(g - 15) * 128, :]

                with nc.Block() as blk:
                    @blk.sync
                    def _(sync):
                        for g in range(NT):
                            b = g % 2
                            if g >= 2:
                                sync.wait_ge(vap.h, g - 1)
                                sync.wait_ge(ssq.h, g - 1)
                            sync.dma_start(out=xt[b][:], in_=xsrc(g)).then_inc(lx[b].h, 16)
                            lx[b].n += 16
                            if g >= 1:
                                sync.wait_ge(vap.h, g)
                                pb = (g - 1) % 2
                                sync.dma_start(
                                    out=xn_sc[(g - 1) * 128:g * 128, :], in_=xnb[pb][:]
                                ).then_inc(xnw[pb].h, 16)
                                xnw[pb].n += 16
                        sync.wait_ge(vap.h, NT)
                        sync.dma_start(out=xn_sc[(NT - 1) * 128:NT * 128, :], in_=xnb[(NT - 1) % 2][:]
                                       ).then_inc(xnw[(NT - 1) % 2].h, 16)
                        xnw[(NT - 1) % 2].n += 16
                        sync.wait_ge(xnw[0].h, xnw[0].n)
                        sync.wait_ge(xnw[1].h, xnw[1].n)

                    @blk.vector
                    def _(vector):
                        for g in range(NT):
                            b = g % 2
                            vector.wait_ge(lx[b].h, 16 * (g // 2 + 1))
                            if g >= 2:
                                vector.wait_ge(xnw[b].h, 16 * (g // 2))
                            vred.inc(vector.reduce_sum(st1[b][:], xt[b][:], axis=AX.X), 1)
                            vector.wait_ge(vred.h, vred.n)
                            vred.inc(vector.tensor_single_scalar(mean[b][:], st1[b][:], float(1.0 / D), ALU.mult), 1)
                            vector.wait_ge(vred.h, vred.n)
                            vector.tensor_mul(msq[b][:], mean[b][:], mean[b][:])
                            vector.wait_ge(ssq.h, g + 1)
                            vred.inc(vector.tensor_single_scalar(v4[b][:], st2[b][:], float(1.0 / D), ALU.mult), 1)
                            vector.wait_ge(vred.h, vred.n)
                            vector.tensor_sub(var[b][:], v4[b][:], msq[b][:]).then_inc(svar.h, 1)
                            vector.wait_ge(srst.h, g + 1)
                            vector.tensor_scalar(
                                xnb[b][:], xt[b][:], mean[b][:, 0:1], rstd[b][:, 0:1],
                                op0=ALU.subtract, op1=ALU.mult,
                            ).then_inc(vap.h, 1)

                    @blk.scalar
                    def _(scalar):
                        for g in range(NT):
                            b = g % 2
                            scalar.wait_ge(lx[b].h, 16 * (g // 2 + 1))
                            scalar.activation(dump[:], xt[b][:], AF.Square,
                                              accum_out=st2[b][:, 0:1]).then_inc(ssq.h, 1)
                            scalar.wait_ge(svar.h, g + 1)
                            sself.inc(scalar.activation(lnv[b][:], var[b][:], AF.Ln, bias=np.float32(EPS)), 1)
                            scalar.wait_ge(sself.h, sself.n)
                            scalar.activation(rstd[b][:], lnv[b][:], AF.Exp,
                                              scale=np.float32(-0.5)).then_inc(srst.h, 1)

        if phase_limit >= 2:
            with ExitStack() as pctx:
                bvrow_bf = pctx.enter_context(nc.sbuf_tensor("bvrow_bf", [1, DH], BF16))
                wk_sb = pctx.enter_context(nc.sbuf_tensor("wk_sb", [128, CC * 1024], BF16))
                wv_sb = pctx.enter_context(nc.sbuf_tensor("wv_sb", [128, CC * 1024], BF16))
                xnt = [pctx.enter_context(nc.sbuf_tensor(f"xnt{i}", [128, CC * TB], BF16)) for i in range(2)]
                ktile = [pctx.enter_context(nc.sbuf_tensor(f"ktile{i}", [128, TB], BF16)) for i in range(4)]
                vtile = [pctx.enter_context(nc.sbuf_tensor(f"vtile{i}", [128, DH], BF16)) for i in range(4)]
                psK = [pctx.enter_context(nc.psum_tensor(f"psK{i}", [128, 512], F32)) for i in range(2)]
                psV = [pctx.enter_context(nc.psum_tensor(f"psV{i}", [128, DH], F32)) for i in range(2)]
                wkv_ld = Sem(nc, pctx, "wkv_ld")
                xl = [Sem(nc, pctx, f"xl{i}") for i in range(2)]
                kmm = Sem(nc, pctx, "kmm")
                vmm = Sem(nc, pctx, "vmm")
                kev = Sem(nc, pctx, "kev")
                vev = Sem(nc, pctx, "vev")
                kst = [Sem(nc, pctx, f"kst{i}") for i in range(4)]
                vst = [Sem(nc, pctx, f"vst{i}") for i in range(4)]

                with nc.Block() as blk:
                    @blk.gpsimd
                    def _(gpsimd):
                        wkv_ld.inc(gpsimd.dma_start(out=bvrow_bf[:], in_=bqkv_row[0:1, 2048:3072]), 16)
                        for cc in range(CC):
                            wkv_ld.inc(gpsimd.dma_start(
                                out=wk_sb[:, cc * 1024:(cc + 1) * 1024],
                                in_=wqkv[cc * 128:(cc + 1) * 128, 1024:2048]), 16)
                        for cc in range(CC):
                            wkv_ld.inc(gpsimd.dma_start(
                                out=wv_sb[:, cc * 1024:(cc + 1) * 1024],
                                in_=wqkv[cc * 128:(cc + 1) * 128, 2048:3072]), 16)

                    @blk.sync
                    def _(sync):
                        for tb in range(NTB + 1):
                            if tb < NTB:
                                b = tb % 2
                                if tb >= 2:
                                    sync.wait_ge(vmm.h, 4 * (tb - 1))
                                for cc in range(CC):
                                    sync.dma_start_transpose(
                                        xnt[b][:, cc * TB:(cc + 1) * TB],
                                        xn_sc[tb * TB:(tb + 1) * TB, cc * 128:(cc + 1) * 128],
                                    ).then_inc(xl[b].h, 16)
                                    xl[b].n += 16
                            if tb >= 1:
                                ptb = tb - 1
                                for dc in range(NDC):
                                    kg = ptb * NDC + dc
                                    sync.wait_ge(kev.h, kg + 1)
                                    sync.dma_start(
                                        out=kT_sc[dc * 128:(dc + 1) * 128, ptb * TB:(ptb + 1) * TB],
                                        in_=ktile[kg % 4][:],
                                    ).then_inc(kst[kg % 4].h, 16)
                                    kst[kg % 4].n += 16
                                for ts in range(4):
                                    vg = ptb * 4 + ts
                                    sync.wait_ge(vev.h, vg + 1)
                                    sync.dma_start(
                                        out=v_sc[ptb * TB + ts * 128:ptb * TB + (ts + 1) * 128, :],
                                        in_=vtile[vg % 4][:],
                                    ).then_inc(vst[vg % 4].h, 16)
                                    vst[vg % 4].n += 16
                        for i in range(4):
                            sync.wait_ge(kst[i].h, kst[i].n)
                            sync.wait_ge(vst[i].h, vst[i].n)

                    @blk.tensor
                    def _(tensor):
                        tensor.wait_ge(wkv_ld.h, 16 * 65)
                        for tb in range(NTB):
                            b = tb % 2
                            tensor.wait_ge(xl[b].h, 16 * CC * (tb // 2 + 1))
                            for dc in range(NDC):
                                kg = tb * NDC + dc
                                if kg >= 2:
                                    tensor.wait_ge(kev.h, kg - 1)
                                for cc in range(CC):
                                    mm = tensor.matmul(
                                        psK[kg % 2][:],
                                        wk_sb[:, cc * 1024 + dc * 128:cc * 1024 + (dc + 1) * 128],
                                        xnt[b][:, cc * TB:(cc + 1) * TB],
                                        start=(cc == 0), stop=(cc == CC - 1),
                                    )
                                kmm.inc(mm, 1)
                            for ts in range(4):
                                vg = tb * 4 + ts
                                if vg >= 2:
                                    tensor.wait_ge(vev.h, vg - 1)
                                for cc in range(CC):
                                    lhs = xnt[b][:, cc * TB + ts * 128:cc * TB + (ts + 1) * 128]
                                    tensor.matmul(psV[vg % 2][:, 0:512], lhs,
                                                  wv_sb[:, cc * 1024:cc * 1024 + 512],
                                                  start=(cc == 0), stop=False)
                                    tensor.matmul(psV[vg % 2][:, 512:1024], lhs,
                                                  wv_sb[:, cc * 1024 + 512:cc * 1024 + 1024],
                                                  start=(cc == 0), stop=False)
                                tensor.matmul(psV[vg % 2][:, 0:512], ones_bf[0:1, :],
                                              bvrow_bf[0:1, 0:512], start=False, stop=True)
                                mm = tensor.matmul(psV[vg % 2][:, 512:1024], ones_bf[0:1, :],
                                                   bvrow_bf[0:1, 512:1024], start=False, stop=True)
                                vmm.inc(mm, 1)

                    @blk.scalar
                    def _(scalar):
                        for kg in range(NTB * NDC):
                            dc = kg % NDC
                            scalar.wait_ge(kmm.h, kg + 1)
                            if kg >= 4:
                                scalar.wait_ge(kst[kg % 4].h, 16 * (kg // 4))
                            kev.inc(scalar.activation(
                                ktile[kg % 4][:], psK[kg % 2][:], AF.Identity,
                                bias=bqkv_sb[:, 8 + dc:9 + dc]), 1)

                    @blk.vector
                    def _(vector):
                        for vg in range(NTB * 4):
                            vector.wait_ge(vmm.h, vg + 1)
                            if vg >= 4:
                                vector.wait_ge(vst[vg % 4].h, 16 * (vg // 4))
                            vev.inc(vector.tensor_copy(vtile[vg % 4][:], psV[vg % 2][:]), 1)

        if phase_limit >= 3:
            with ExitStack() as pctx:
                wq_sb = pctx.enter_context(nc.sbuf_tensor("wq_sb", [128, CC * 1024], BF16))
                xnt3 = [pctx.enter_context(nc.sbuf_tensor(f"xnt3_{i}", [128, CC * TB], BF16)) for i in range(2)]
                qtile = [pctx.enter_context(nc.sbuf_tensor(f"qtile{i}", [128, TB], BF16)) for i in range(4)]
                psQ = [pctx.enter_context(nc.psum_tensor(f"psQ{i}", [128, 512], F32)) for i in range(2)]
                wq_ld = Sem(nc, pctx, "wq_ld")
                x3 = [Sem(nc, pctx, f"x3_{i}") for i in range(2)]
                qmm = Sem(nc, pctx, "qmm")
                qev = Sem(nc, pctx, "qev")
                qst = [Sem(nc, pctx, f"qst{i}") for i in range(4)]

                with nc.Block() as blk:
                    @blk.gpsimd
                    def _(gpsimd):
                        for cc in range(CC):
                            wq_ld.inc(gpsimd.dma_start(
                                out=wq_sb[:, cc * 1024:(cc + 1) * 1024],
                                in_=wqkv[cc * 128:(cc + 1) * 128, 0:1024]), 16)

                    @blk.sync
                    def _(sync):
                        for tb in range(NTB_OWN + 1):
                            if tb < NTB_OWN:
                                b = tb % 2
                                if tb >= 2:
                                    sync.wait_ge(qmm.h, NDC * (tb - 1))
                                for cc in range(CC):
                                    sync.dma_start_transpose(
                                        xnt3[b][:, cc * TB:(cc + 1) * TB],
                                        xn_sc[tb * TB:(tb + 1) * TB, cc * 128:(cc + 1) * 128],
                                    ).then_inc(x3[b].h, 16)
                                    x3[b].n += 16
                            if tb >= 1:
                                ptb = tb - 1
                                for dc in range(NDC):
                                    qg = ptb * NDC + dc
                                    sync.wait_ge(qev.h, qg + 1)
                                    sync.dma_start(
                                        out=qT_sc[dc * 128:(dc + 1) * 128, ptb * TB:(ptb + 1) * TB],
                                        in_=qtile[qg % 4][:],
                                    ).then_inc(qst[qg % 4].h, 16)
                                    qst[qg % 4].n += 16
                        for i in range(4):
                            sync.wait_ge(qst[i].h, qst[i].n)

                    @blk.tensor
                    def _(tensor):
                        tensor.wait_ge(wq_ld.h, 16 * CC)
                        for tb in range(NTB_OWN):
                            b = tb % 2
                            tensor.wait_ge(x3[b].h, 16 * CC * (tb // 2 + 1))
                            for dc in range(NDC):
                                qg = tb * NDC + dc
                                if qg >= 2:
                                    tensor.wait_ge(qev.h, qg - 1)
                                for cc in range(CC):
                                    mm = tensor.matmul(
                                        psQ[qg % 2][:],
                                        wq_sb[:, cc * 1024 + dc * 128:cc * 1024 + (dc + 1) * 128],
                                        xnt3[b][:, cc * TB:(cc + 1) * TB],
                                        start=(cc == 0), stop=(cc == CC - 1),
                                    )
                                qmm.inc(mm, 1)

                    @blk.scalar
                    def _(scalar):
                        for qg in range(NTB_OWN * NDC):
                            dc = qg % NDC
                            scalar.wait_ge(qmm.h, qg + 1)
                            if qg >= 4:
                                scalar.wait_ge(qst[qg % 4].h, 16 * (qg // 4))
                            qev.inc(scalar.activation(
                                qtile[qg % 4][:], psQ[qg % 2][:], AF.Identity,
                                bias=bq_s[:, dc:dc + 1], scale=np.float32(ASCL)), 1)

        if phase_limit >= 4:
            kT_r = kT_sc.ap().rearrange("(dc p) k -> p dc k", p=128)
            v_r = v_sc.ap().rearrange("(g p) d -> p g d", p=128)
            qT_r = qT_sc.ap().rearrange("(dc p) q -> p dc q", p=128)
            NG, QG, NCH, CHK = 2, 8, 8, 1024
            NS = NG * NCH * QG
            titP = {}
            titO = {}
            _t = 0
            for _g in range(NG):
                for _m in range(_g * 64, (_g + 1) * 64):
                    titP[_m] = _t
                    _t += 8
                for _q in range(_g * 8, (_g + 1) * 8):
                    titO[_q] = _t
                    _t += 8
            apr_cum = []
            _a = 0
            for _n in range(NS):
                if (_n // 8) % NCH > 0:
                    _a += 1
                apr_cum.append(_a)

            with ExitStack() as pctx:
                bo_bf = pctx.enter_context(nc.sbuf_tensor("bo_bf", [1, D], BF16))
                wo_sb = pctx.enter_context(nc.sbuf_tensor("wo_sb", [128, NDC * D], BF16))
                q_sb = pctx.enter_context(nc.sbuf_tensor("q_sb", [128, NDC * 1024], BF16))
                kt_sb = [pctx.enter_context(nc.sbuf_tensor(f"kt_sb{i}", [128, NDC * CHK], BF16)) for i in range(2)]
                v_sb = [pctx.enter_context(nc.sbuf_tensor(f"v_sb{i}", [128, 8 * DH], BF16)) for i in range(2)]
                P_sb = [pctx.enter_context(nc.sbuf_tensor(f"P_sb{i}", [128, CHK], BF16)) for i in range(2)]
                pt_all = pctx.enter_context(nc.sbuf_tensor("pt_all", [128, 8 * 128], BF16))
                pt_sb = [pt_all[:, i * 128:(i + 1) * 128] for i in range(8)]
                O_sb = pctx.enter_context(nc.sbuf_tensor("O_sb", [128, QG * DH], F32))
                obf = [pctx.enter_context(nc.sbuf_tensor(f"obf{i}", [128, DH], BF16)) for i in range(2)]
                ot_sb = [pctx.enter_context(nc.sbuf_tensor(f"ot_sb{i}", [128, DH], BF16)) for i in range(2)]
                stat = pctx.enter_context(nc.sbuf_tensor("stat", [128, 8 * QG], F32))
                negm_pp = stat[:, 0:2 * QG]
                mtmp = stat[:, 2 * QG:3 * QG]
                db = stat[:, 3 * QG:4 * QG]
                alb = stat[:, 4 * QG:5 * QG]
                lcb = stat[:, 5 * QG:6 * QG]
                lsum_run = stat[:, 6 * QG:7 * QG]
                rlb = stat[:, 7 * QG:8 * QG]
                ostg = [pctx.enter_context(nc.sbuf_tensor(f"ostg{i}", [128, 512], F32)) for i in range(2)]
                psS = [pctx.enter_context(nc.psum_tensor(f"psS{i}", [128, CHK], F32)) for i in range(2)]
                psO = pctx.enter_context(nc.psum_tensor("psO", [128, DH], F32))
                psPT = [pctx.enter_context(nc.psum_tensor(f"psPT{i}", [128, 1024], BF16)) for i in range(2)]

                def slotPR(pg):
                    return psS[(pg % 4) // 2][:, (pg % 2) * 512:(pg % 2 + 1) * 512]

                wol = Sem(nc, pctx, "wol")
                qld = Sem(nc, pctx, "qld")
                kld = [Sem(nc, pctx, f"kld{i}") for i in range(2)]
                vld = [Sem(nc, pctx, f"vld{i}") for i in range(2)]
                smm = Sem(nc, pctx, "smm")
                dvp = Sem(nc, pctx, "dvp")
                aprep = Sem(nc, pctx, "aprep")
                pexp = Sem(nc, pctx, "pexp")
                ptm = Sem(nc, pctx, "ptm")
                ptc = Sem(nc, pctx, "ptc")
                pvm = Sem(nc, pctx, "pvm")
                oacc = Sem(nc, pctx, "oacc")
                oev = Sem(nc, pctx, "oev")
                pmm = Sem(nc, pctx, "pmm")
                pev = Sem(nc, pctx, "pev")
                past = [Sem(nc, pctx, f"past{i}") for i in range(2)]
                vsf = Sem(nc, pctx, "vsf")

                with nc.Block() as blk:
                    @blk.gpsimd
                    def _(gpsimd):
                        wol.inc(gpsimd.dma_start(out=bo_bf[:], in_=bo_row[:, :]), 16)
                        for lc in range(NG * NCH):
                            c = lc % NCH
                            if lc >= 2:
                                gpsimd.wait_ge(pvm.h, (lc - 1) * 64)
                            vld[lc % 2].inc(gpsimd.dma_start(
                                out=v_sb[lc % 2].ap().rearrange("p (g d) -> p g d", d=DH),
                                in_=v_r[:, c * 8:(c + 1) * 8, :]), 16)
                            if 2 <= lc < 2 + NDC:
                                dc = lc - 2
                                wol.inc(gpsimd.dma_start(
                                    out=wo_sb[:, dc * D:(dc + 1) * D],
                                    in_=wo[dc * 128:(dc + 1) * 128, :]), 16)

                    @blk.sync
                    def _(sync):
                        def kt_load(lc):
                            c = lc % NCH
                            if lc >= 2:
                                sync.wait_ge(smm.h, (lc - 1) * 8)
                            kld[lc % 2].inc(sync.dma_start(
                                out=kt_sb[lc % 2].ap().rearrange("p (dc k) -> p dc k", k=CHK),
                                in_=kT_r[:, :, c * CHK:(c + 1) * CHK]), 16)

                        def tail_sync(g):
                            for qt in range(QG):
                                q_i = g * 8 + qt
                                for ob in range(NOB):
                                    pg = q_i * 8 + ob
                                    sync.wait_ge(pev.h, pg + 1)
                                    past[pg % 2].inc(sync.dma_start(
                                        out=pa_sc[q_i * 128:(q_i + 1) * 128,
                                                  ob * 512:(ob + 1) * 512],
                                        in_=ostg[pg % 2][:]), 16)

                        qld.inc(sync.dma_start(
                            out=q_sb.ap().rearrange("p (dc q) -> p dc q", q=1024),
                            in_=qT_r[:, :, 0:1024]), 16)
                        for lc in range(0, 8):
                            kt_load(lc)
                        sync.wait_ge(smm.h, 64)
                        qld.inc(sync.dma_start(
                            out=q_sb.ap().rearrange("p (dc q) -> p dc q", q=1024),
                            in_=qT_r[:, :, 1024:2048]), 16)
                        kt_load(8)
                        kt_load(9)
                        tail_sync(0)
                        for lc in range(10, 16):
                            kt_load(lc)
                        tail_sync(1)
                        sync.wait_ge(past[0].h, past[0].n)
                        sync.wait_ge(past[1].h, past[1].n)

                    @blk.tensor
                    def _(tensor):
                        def pv_block(n_p):
                            lc_p = n_p // 8
                            pb = n_p % 2
                            t0 = titP[n_p]
                            tensor.wait_ge(vld[lc_p % 2].h, 16 * (lc_p // 2 + 1))
                            tensor.wait_ge(oacc.h, n_p)
                            tensor.wait_ge(pexp.h, n_p + 1)
                            for kc in range(9):
                                if kc < 8:
                                    t = t0 + kc
                                    if t >= 2:
                                        tensor.wait_ge(ptc.h, t - 1)
                                    ptm.inc(tensor.transpose(
                                        psPT[t % 2][:, 0:128],
                                        P_sb[pb][:, kc * 128:(kc + 1) * 128],
                                        ident_bf[:]), 1)
                                if kc >= 1:
                                    pc = kc - 1
                                    tensor.wait_ge(ptc.h, t0 + pc + 1)
                                    tensor.matmul(psO[:, 0:512], pt_sb[pc % 8][:],
                                                  v_sb[lc_p % 2][:, pc * DH:pc * DH + 512],
                                                  start=(pc == 0), stop=(pc == 7))
                                    mm = tensor.matmul(psO[:, 512:1024], pt_sb[pc % 8][:],
                                                       v_sb[lc_p % 2][:, pc * DH + 512:(pc + 1) * DH],
                                                       start=(pc == 0), stop=(pc == 7))
                                    pvm.inc(mm, 1)

                        for g in range(NG):
                            for c in range(NCH):
                                lc = g * NCH + c
                                tensor.wait_ge(kld[lc % 2].h, 16 * (lc // 2 + 1))
                                if c == 0:
                                    tensor.wait_ge(qld.h, 16 * (g + 1))
                                for qt in range(QG):
                                    n_s = lc * 8 + qt
                                    b = n_s % 2
                                    if n_s >= 2:
                                        tensor.wait_ge(pexp.h, n_s - 1)
                                    if g == 1 and c == 0 and qt < 2:
                                        tensor.wait_ge(pev.h, 62 if qt == 0 else 64)
                                    for dc in range(NDC):
                                        q_lhsT = q_sb[:, dc * 1024 + qt * 128:dc * 1024 + (qt + 1) * 128]
                                        tensor.matmul(
                                            psS[b][:, 0:512], q_lhsT,
                                            kt_sb[lc % 2][:, dc * CHK:dc * CHK + 512],
                                            start=(dc == 0), stop=(dc == NDC - 1))
                                        mm = tensor.matmul(
                                            psS[b][:, 512:1024], q_lhsT,
                                            kt_sb[lc % 2][:, dc * CHK + 512:(dc + 1) * CHK],
                                            start=(dc == 0), stop=(dc == NDC - 1))
                                    smm.inc(mm, 1)
                                    if n_s - 1 >= g * 64:
                                        pv_block(n_s - 1)
                            pv_block((g + 1) * 64 - 1)
                            if g == 0:
                                tensor.wait_ge(wol.h, 16 * (NDC + 1))
                            for qt in range(QG):
                                q_i = g * 8 + qt
                                tensor.wait_ge(oev.h, q_i + 1)
                                t0 = titO[q_i]
                                for dc in range(NDC):
                                    t = t0 + dc
                                    tensor.wait_ge(ptc.h, t - 1)
                                    ptm.inc(tensor.transpose(
                                        psPT[t % 2][:, 0:128],
                                        obf[qt % 2][:, dc * 128:(dc + 1) * 128],
                                        ident_bf[:]), 1)
                                tensor.wait_ge(ptc.h, t0 + 8)
                                for ob in range(NOB):
                                    pg = q_i * 8 + ob
                                    if pg >= 4:
                                        tensor.wait_ge(pev.h, pg - 3)
                                    for dc in range(NDC):
                                        tensor.matmul(
                                            slotPR(pg),
                                            ot_sb[qt % 2][:, dc * 128:(dc + 1) * 128],
                                            wo_sb[:, dc * D + ob * 512:dc * D + (ob + 1) * 512],
                                            start=(dc == 0), stop=False)
                                    mm = tensor.matmul(slotPR(pg), ones_bf[0:1, :],
                                                       bo_bf[0:1, ob * 512:(ob + 1) * 512],
                                                       start=False, stop=True)
                                    pmm.inc(mm, 1)

                    @blk.scalar
                    def _(scalar):
                        def pt_copies(m):
                            t0 = titP[m]
                            for kc in range(8):
                                t = t0 + kc
                                scalar.wait_ge(ptm.h, t + 1)
                                if m >= 1:
                                    scalar.wait_ge(pvm.h, (m - 1) * 8 + kc + 1)
                                ptc.inc(scalar.copy(pt_sb[kc][:], psPT[t % 2][:, 0:128]), 1)

                        for g in range(NG):
                            for c in range(NCH):
                                lc = g * NCH + c
                                for qt in range(QG):
                                    n_s = lc * 8 + qt
                                    if n_s - 1 >= g * 64:
                                        pt_copies(n_s - 1)
                                    scalar.wait_ge(dvp.h, n_s + 1)
                                    if n_s >= 2:
                                        scalar.wait_ge(ptm.h, titP[n_s - 2] + 8)
                                    if c > 0:
                                        aprep.inc(scalar.activation(
                                            alb[:, qt:qt + 1], db[:, qt:qt + 1], AF.Exp), 1)
                                    acc = lcb[:, qt:qt + 1] if c > 0 else lsum_run[:, qt:qt + 1]
                                    pexp.inc(scalar.activation(
                                        P_sb[n_s % 2][:], psS[n_s % 2][:], AF.Exp,
                                        bias=negm_pp[:, 2 * qt + (c % 2):2 * qt + (c % 2) + 1],
                                        accum_out=acc), 1)
                            pt_copies((g + 1) * 64 - 1)
                            for qt in range(QG):
                                q_i = g * 8 + qt
                                t0 = titO[q_i]
                                for dc in range(NDC):
                                    t = t0 + dc
                                    scalar.wait_ge(ptm.h, t + 1)
                                    if q_i >= 2:
                                        scalar.wait_ge(pmm.h, (q_i - 1) * 8)
                                    ptc.inc(scalar.copy(
                                        ot_sb[qt % 2][:, dc * 128:(dc + 1) * 128],
                                        psPT[t % 2][:, 0:128]), 1)

                    @blk.vector
                    def _(vector):
                        vsf_n = [0]

                        def vself(instr):
                            vsf_n[0] += 1
                            instr.then_inc(vsf.h, 1)
                            vector.wait_ge(vsf.h, vsf_n[0])

                        for g in range(NG):
                            for c in range(NCH):
                                lc = g * NCH + c
                                for qt in range(QG):
                                    n_s = lc * 8 + qt
                                    b = n_s % 2
                                    vector.wait_ge(smm.h, n_s + 1)
                                    cn = 2 * qt + c % 2
                                    co = 2 * qt + (c + 1) % 2
                                    if c == 0:
                                        dvp.inc(vector.reduce_max(
                                            negm_pp[:, cn:cn + 1], psS[b][:],
                                            axis=AX.X, negate=True), 1)
                                    else:
                                        vself(vector.reduce_max(
                                            mtmp[:, qt:qt + 1], psS[b][:],
                                            axis=AX.X, negate=True))
                                        vself(vector.tensor_tensor(
                                            negm_pp[:, cn:cn + 1], negm_pp[:, co:co + 1],
                                            mtmp[:, qt:qt + 1], op=ALU.min))
                                        dvp.inc(vector.tensor_sub(
                                            db[:, qt:qt + 1], negm_pp[:, cn:cn + 1],
                                            negm_pp[:, co:co + 1]), 1)
                                    if c > 0:
                                        vector.wait_ge(pexp.h, n_s + 1)
                                        vector.wait_ge(aprep.h, apr_cum[n_s])
                                        lupd = vector.tensor_scalar(
                                            lsum_run[:, qt:qt + 1], lsum_run[:, qt:qt + 1],
                                            alb[:, qt:qt + 1], lcb[:, qt:qt + 1],
                                            op0=ALU.mult, op1=ALU.add)
                                        if c == NCH - 1:
                                            vsf_n[0] += 1
                                            lupd.then_inc(vsf.h, 1)
                                    vector.wait_ge(pvm.h, (n_s + 1) * 8)
                                    ov = O_sb[:, qt * DH:(qt + 1) * DH]
                                    if c == 0:
                                        oacc.inc(vector.tensor_copy(ov, psO[:]), 1)
                                    else:
                                        oacc.inc(vector.scalar_tensor_tensor(
                                            ov, ov, alb[:, qt:qt + 1], psO[:],
                                            op0=ALU.mult, op1=ALU.add), 1)
                            vector.wait_ge(vsf.h, vsf_n[0])

                            def evac_qt(qt):
                                q_i = g * 8 + qt
                                for ob in range(NOB):
                                    pg = q_i * 8 + ob
                                    vector.wait_ge(pmm.h, pg + 1)
                                    if pg >= 2:
                                        vector.wait_ge(past[pg % 2].h, 16 * (pg // 2))
                                    pev.inc(vector.tensor_copy(
                                        ostg[pg % 2][:], slotPR(pg)), 1)

                            for qt in range(QG):
                                q_i = g * 8 + qt
                                vself(vector.reciprocal(rlb[:, qt:qt + 1],
                                                        lsum_run[:, qt:qt + 1]))
                                if q_i >= 2:
                                    vector.wait_ge(ptm.h, titO[q_i - 2] + 8)
                                oev.inc(vector.tensor_scalar_mul(
                                    obf[qt % 2][:], O_sb[:, qt * DH:(qt + 1) * DH],
                                    rlb[:, qt:qt + 1]), 1)
                                if qt >= 1:
                                    evac_qt(qt - 1)
                            evac_qt(QG - 1)

        if phase_limit >= 5:
            wgT_r = wgT.ap().rearrange("(cc p) i -> p cc i", p=128)
            wuT_r = wuT.ap().rearrange("(cc p) i -> p cc i", p=128)
            with ExitStack() as pctx:
                xno = pctx.enter_context(nc.sbuf_tensor("xno", [128, CC * TOWN], BF16))
                wg_sb = [pctx.enter_context(nc.sbuf_tensor(f"wg_sb{i}", [128, CC * 128], BF16)) for i in range(2)]
                wu_sb = [pctx.enter_context(nc.sbuf_tensor(f"wu_sb{i}", [128, CC * 128], BF16)) for i in range(2)]
                silu_sb = [pctx.enter_context(nc.sbuf_tensor(f"silu{i}", [128, 512], BF16)) for i in range(2)]
                h_sb = [pctx.enter_context(nc.sbuf_tensor(f"h_sb{i}", [128, TOWN], BF16)) for i in range(2)]
                psG = [pctx.enter_context(nc.psum_tensor(f"psG{i}", [128, 512], F32)) for i in range(4)]
                psU = [pctx.enter_context(nc.psum_tensor(f"psU{i}", [128, 512], F32)) for i in range(4)]
                xnol = Sem(nc, pctx, "xnol")
                wl = [Sem(nc, pctx, f"wl{i}") for i in range(2)]
                gmm = Sem(nc, pctx, "gmm")
                sil = Sem(nc, pctx, "sil")
                hev = Sem(nc, pctx, "hev")
                hst = [Sem(nc, pctx, f"hst{i}") for i in range(2)]

                with nc.Block() as blk:
                    @blk.sync
                    def _(sync):
                        for cc in range(CC):
                            sync.dma_start_transpose(
                                xno[:, cc * TOWN:(cc + 1) * TOWN],
                                xn_sc[0:TOWN, cc * 128:(cc + 1) * 128],
                            ).then_inc(xnol.h, 16)
                            xnol.n += 16
                        for ic in range(NIC):
                            sync.wait_ge(hev.h, (ic + 1) * NTBF)
                            sync.dma_start(out=h_sc[ic * 128:(ic + 1) * 128, :],
                                           in_=h_sb[ic % 2][:]).then_inc(hst[ic % 2].h, 16)
                            hst[ic % 2].n += 16
                        for i in range(2):
                            sync.wait_ge(hst[i].h, hst[i].n)

                    @blk.gpsimd
                    def _(gpsimd):
                        ccr = [(0, 8), (8, 16), (16, 24), (24, 32)]
                        for ic in range(NIC):
                            b = ic % 2
                            if ic >= 2:
                                gpsimd.wait_ge(gmm.h, NTBF * (ic - 1))
                            for (c0, c1) in ccr:
                                wl[b].inc(gpsimd.dma_start(
                                    out=wg_sb[b].ap().rearrange("p (cc i) -> p cc i", i=128)[:, c0:c1, :],
                                    in_=wgT_r[:, c0:c1, ic * 128:(ic + 1) * 128]), 16)
                                wl[b].inc(gpsimd.dma_start(
                                    out=wu_sb[b].ap().rearrange("p (cc i) -> p cc i", i=128)[:, c0:c1, :],
                                    in_=wuT_r[:, c0:c1, ic * 128:(ic + 1) * 128]), 16)

                    @blk.tensor
                    def _(tensor):
                        tensor.wait_ge(xnol.h, 16 * CC)
                        for ic in range(NIC):
                            b = ic % 2
                            tensor.wait_ge(wl[b].h, 16 * 8 * (ic // 2 + 1))
                            for tf in range(NTBF):
                                gg = ic * NTBF + tf
                                if gg >= 4:
                                    tensor.wait_ge(hev.h, gg - 3)
                                for cc in range(CC):
                                    rhs = xno[:, cc * TOWN + tf * 512:cc * TOWN + (tf + 1) * 512]
                                    tensor.matmul(psG[tf][:],
                                                  wg_sb[b][:, cc * 128:(cc + 1) * 128], rhs,
                                                  start=(cc == 0), stop=(cc == CC - 1))
                                    mm = tensor.matmul(psU[tf][:],
                                                       wu_sb[b][:, cc * 128:(cc + 1) * 128], rhs,
                                                       start=(cc == 0), stop=(cc == CC - 1))
                                gmm.inc(mm, 1)

                    @blk.scalar
                    def _(scalar):
                        for ic in range(NIC):
                            for tf in range(NTBF):
                                gg = ic * NTBF + tf
                                scalar.wait_ge(gmm.h, gg + 1)
                                if gg >= 2:
                                    scalar.wait_ge(hev.h, gg - 1)
                                sil.inc(scalar.activation(
                                    silu_sb[gg % 2][:], psG[tf][:], AF.Silu,
                                    scale=sg_sb[:, ic:ic + 1]), 1)

                    @blk.vector
                    def _(vector):
                        for ic in range(NIC):
                            for tf in range(NTBF):
                                gg = ic * NTBF + tf
                                vector.wait_ge(sil.h, gg + 1)
                                if ic >= 2 and tf == 0:
                                    vector.wait_ge(hst[ic % 2].h, 16 * (ic // 2))
                                hev.inc(vector.scalar_tensor_tensor(
                                    h_sb[ic % 2][:, tf * 512:(tf + 1) * 512],
                                    psU[tf][:], su_sb[:, ic:ic + 1], silu_sb[gg % 2][:],
                                    op0=ALU.mult, op1=ALU.mult), 1)

        if phase_limit >= 6:
            h_r = h_sc.ap().rearrange("(ic p) t -> p ic t", p=128)
            with ExitStack() as pctx:
                h_res = pctx.enter_context(nc.sbuf_tensor("h_res", [128, NIC * TG], BF16))
                wd_sb = [pctx.enter_context(nc.sbuf_tensor(f"wd_sb{i}", [128, 512], BF16)) for i in range(6)]
                sd_sb = [pctx.enter_context(nc.sbuf_tensor(f"sd_sb{i}", [128, 512], F32)) for i in range(2)]
                pa_sb = [pctx.enter_context(nc.sbuf_tensor(f"pa_sb{i}", [128, 512], F32)) for i in range(3)]
                xt_sb = [pctx.enter_context(nc.sbuf_tensor(f"xt_sb{i}", [128, 512], F32)) for i in range(3)]
                tmp_sb = [pctx.enter_context(nc.sbuf_tensor(f"tmp_sb{i}", [128, 512], F32)) for i in range(2)]
                out_sb = [pctx.enter_context(nc.sbuf_tensor(f"out_sb{i}", [128, 512], F32)) for i in range(3)]
                psD = pctx.enter_context(nc.psum_tensor("psD", [128, NTS * 512], F32))
                hrl = Sem(nc, pctx, "hrl")
                wdl6 = [Sem(nc, pctx, f"wdl6_{i}") for i in range(6)]
                sdl = [Sem(nc, pctx, f"sdl{i}") for i in range(2)]
                pal = [Sem(nc, pctx, f"pal{i}") for i in range(3)]
                xtl = [Sem(nc, pctx, f"xtl{i}") for i in range(3)]
                dic = Sem(nc, pctx, "dic")
                dmm = Sem(nc, pctx, "dmm")
                dev = Sem(nc, pctx, "dev")
                outst = [Sem(nc, pctx, f"outst{i}") for i in range(3)]
                vs6 = Sem(nc, pctx, "vs6")

                ICR = [(0, 11), (11, 22), (22, 33), (33, 44), (44, 55), (55, 66), (66, 76), (76, 86)]

                with nc.Block() as blk:
                    @blk.sync
                    def _(sync):
                        for tg in range(NTG):
                            for (i0, i1) in ICR:
                                if tg >= 1:
                                    sync.wait_ge(dic.h, (tg * NOB - 1) * NIC + i1)
                                sync.dma_start(
                                    out=h_res.ap().rearrange("p (ic t) -> p ic t", t=TG)[:, i0:i1, :],
                                    in_=h_r[:, i0:i1, tg * TG:(tg + 1) * TG],
                                ).then_inc(hrl.h, 16)
                                hrl.n += 16


                    @blk.scalar
                    def _(scalar):
                        def out_store(k):
                            ktg, krem = k // (NOB * NTS), k % (NOB * NTS)
                            kob, kts = krem // NTS, krem % NTS
                            scalar.dma_start(
                                out=out[ktg * TG + kts * 128:ktg * TG + (kts + 1) * 128,
                                        kob * 512:(kob + 1) * 512],
                                in_=out_sb[k % 3][:],
                            ).then_inc(outst[k % 3].h, 16)
                            outst[k % 3].n += 16

                        for tg in range(NTG):
                            for ob in range(NOB):
                                tgob = tg * NOB + ob
                                if tgob >= 2:
                                    scalar.wait_ge(dev.h, (tgob - 1) * NTS)
                                scalar.dma_start(
                                    out=sd_sb[ob % 2][:],
                                    in_=sd_row[0:1, ob * 512:(ob + 1) * 512].broadcast_to([128, 512]),
                                ).then_inc(sdl[ob % 2].h, 16)
                                sdl[ob % 2].n += 16
                                for ts in range(NTS):
                                    pai = tgob * NTS + ts
                                    if pai >= 3:
                                        scalar.wait_ge(dev.h, pai - 2)
                                        out_store(pai - 3)
                                    scalar.dma_start(
                                        out=pa_sb[pai % 3][:],
                                        in_=pa_sc[tg * TG + ts * 128:tg * TG + (ts + 1) * 128,
                                                  ob * 512:(ob + 1) * 512],
                                    ).then_inc(pal[pai % 3].h, 16)
                                    pal[pai % 3].n += 16
                                    scalar.dma_start(
                                        out=xt_sb[pai % 3][:],
                                        in_=x_own[tg * TG + ts * 128:tg * TG + (ts + 1) * 128,
                                                  ob * 512:(ob + 1) * 512],
                                    ).then_inc(xtl[pai % 3].h, 16)
                                    xtl[pai % 3].n += 16
                        npai = NTG * NOB * NTS
                        for k in range(npai - 3, npai):
                            scalar.wait_ge(dev.h, k + 1)
                            out_store(k)
                        for i in range(3):
                            scalar.wait_ge(outst[i].h, outst[i].n)

                    @blk.gpsimd
                    def _(gpsimd):
                        for tg in range(NTG):
                            for ob in range(NOB):
                                for ic in range(NIC):
                                    gl = (tg * NOB + ob) * NIC + ic
                                    if gl >= 6:
                                        gpsimd.wait_ge(dic.h, gl - 5)
                                    wdl6[gl % 6].inc(gpsimd.dma_start(
                                        out=wd_sb[gl % 6][:],
                                        in_=wdT[ic * 128:(ic + 1) * 128,
                                                ob * 512:(ob + 1) * 512]), 16)

                    @blk.tensor
                    def _(tensor):
                        for tg in range(NTG):
                            tensor.wait_ge(hrl.h, 16 * 8 * (tg + 1))
                            for ob in range(NOB):
                                tgob = tg * NOB + ob
                                for ic in range(NIC):
                                    gl = tgob * NIC + ic
                                    b = gl % 6
                                    tensor.wait_ge(wdl6[gl % 6].h, 16 * (gl // 6 + 1))
                                    for ts in range(NTS):
                                        if tgob >= 1 and ic == 0:
                                            tensor.wait_ge(dev.h, (tgob - 1) * NTS + ts + 1)
                                        mm = tensor.matmul(
                                            psD[:, ts * 512:(ts + 1) * 512],
                                            h_res[:, ic * TG + ts * 128:ic * TG + (ts + 1) * 128],
                                            wd_sb[b][:],
                                            start=(ic == 0), stop=(ic == NIC - 1),
                                        )
                                        if ts == NTS - 1:
                                            dic.inc(mm, 1)
                                        elif ic == NIC - 1:
                                            dmm.inc(mm, 1)

                    @blk.vector
                    def _(vector):
                        for tg in range(NTG):
                            for ob in range(NOB):
                                tgob = tg * NOB + ob
                                vector.wait_ge(sdl[ob % 2].h, 16 * (tgob // 2 + 1))
                                for ts in range(NTS):
                                    pai = tgob * NTS + ts
                                    if ts < NTS - 1:
                                        vector.wait_ge(dmm.h, tgob * (NTS - 1) + ts + 1)
                                    else:
                                        vector.wait_ge(dic.h, (tgob + 1) * NIC)
                                    vector.wait_ge(pal[pai % 3].h, 16 * (pai // 3 + 1))
                                    vector.wait_ge(xtl[pai % 3].h, 16 * (pai // 3 + 1))
                                    if pai >= 3:
                                        vector.wait_ge(outst[pai % 3].h, 16 * (pai // 3))
                                    vs6.inc(vector.tensor_mul(tmp_sb[pai % 2][:],
                                                      psD[:, ts * 512:(ts + 1) * 512],
                                                      sd_sb[ob % 2][:]), 1)
                                    vector.wait_ge(vs6.h, vs6.n)
                                    vs6.inc(vector.tensor_add(tmp_sb[pai % 2][:],
                                                              tmp_sb[pai % 2][:],
                                                              pa_sb[pai % 3][:]), 1)
                                    vector.wait_ge(vs6.h, vs6.n)
                                    dev.inc(vector.tensor_add(out_sb[pai % 3][:],
                                                              tmp_sb[pai % 2][:],
                                                              xt_sb[pai % 3][:]), 1)

    return nc


BF = ml_dtypes.bfloat16


def kernel(L0_attn_x, L0_attn_Wqkv, L0_attn_bqkv, L0_attn_Wo, L0_attn_bo,
           L0_ffn_w_gate, L0_ffn_s_gate, L0_ffn_w_up, L0_ffn_s_up,
           L0_ffn_w_down, L0_ffn_s_down, _core_ids=None, _dbg=False,
           _phase_limit=PHASE_LIMIT):
    x = np.ascontiguousarray(np.asarray(L0_attn_x, dtype=np.float32))
    wqkv = np.ascontiguousarray(np.asarray(L0_attn_Wqkv, dtype=np.float32))
    bqkv = np.asarray(L0_attn_bqkv, dtype=np.float32)
    wo = np.ascontiguousarray(np.asarray(L0_attn_Wo, dtype=np.float32))
    bo = np.asarray(L0_attn_bo, dtype=np.float32)
    wgT = np.ascontiguousarray(np.asarray(L0_ffn_w_gate, dtype=np.int32).T)
    wuT = np.ascontiguousarray(np.asarray(L0_ffn_w_up, dtype=np.int32).T)
    wdT = np.ascontiguousarray(np.asarray(L0_ffn_w_down, dtype=np.int32).T)
    sg = np.asarray(L0_ffn_s_gate, dtype=np.float32)
    su = np.asarray(L0_ffn_s_up, dtype=np.float32)
    sd = np.asarray(L0_ffn_s_down, dtype=np.float32)

    bqkv_pp = np.ascontiguousarray(bqkv.reshape(24, 128).T)
    bqkv_row = np.ascontiguousarray(bqkv.reshape(1, FQ))
    bo_row = np.ascontiguousarray(bo.reshape(1, D))
    sg_pp = np.ascontiguousarray(sg.reshape(NIC, 128).T)
    su_pp = np.ascontiguousarray(su.reshape(NIC, 128).T)
    sd_row = np.ascontiguousarray(sd.reshape(1, D))
    ones_in = np.ones((1, 128), dtype=BF)
    ident_in = np.eye(128, dtype=BF)

    core_ids = list(range(8)) if _core_ids is None else _core_ids
    in_maps = []
    for c in core_ids:
        b, r = c // 4, c % 4
        xb = x[b]
        own = np.ascontiguousarray(xb[r * TOWN:(r + 1) * TOWN])
        rest = np.ascontiguousarray(np.concatenate([xb[:r * TOWN], xb[(r + 1) * TOWN:]], axis=0))
        in_maps.append({
            "x_own": own, "x_rest": rest, "wqkv": wqkv, "bqkv_pp": bqkv_pp,
            "bqkv_row": bqkv_row, "wo": wo, "bo_row": bo_row,
            "wgT": wgT, "wuT": wuT, "wdT": wdT,
            "sg_pp": sg_pp, "su_pp": su_pp, "sd_row": sd_row,
            "ones_in": ones_in, "ident_in": ident_in,
        })

    nc = build_nc(dbg=_dbg, phase_limit=_phase_limit)
    res = run_bass_kernel_spmd(nc, in_maps, core_ids=list(range(len(core_ids))))
    if _dbg:
        return res
    shards = [res.results[i]["out"] for i in range(len(core_ids))]
    full = np.concatenate(shards, axis=0)
    return full.reshape(2, BTOK, D)


# version 08 (final)
# speedup vs baseline: 1.1685x; 1.1685x over previous
import numpy as np
import ml_dtypes
from contextlib import ExitStack

import concourse.bass as bass
import concourse.mybir as mybir
from concourse.bass_utils import run_bass_kernel_spmd

F32 = mybir.dt.float32
BF16 = mybir.dt.bfloat16
I32 = mybir.dt.int32
AF = mybir.ActivationFunctionType
ALU = mybir.AluOpType
AX = mybir.AxisListType

D = 4096
CC = 32
BTOK = 8192
TOWN = 2048
DH = 1024
NDC = 8
FQ = 3072
INTER = 11008
NIC = 86
EPS = 1e-5
ASCL = 0.03125

NT = 64
TB = 512
NTB = 16
NTB_OWN = 4
NQT = 16
NKB = 16
NKC = 64
NTBF = 4
TG = 1024
NTG = 2
NOB = 8
NTS = 8

PHASE_LIMIT = 6
DBG = False


class Sem:
    __slots__ = ("h", "n")

    def __init__(self, nc, ctx, name):
        self.h = ctx.enter_context(nc.semaphore(name))
        self.n = 0

    def inc(self, instr, k=1):
        instr.then_inc(self.h, k)
        self.n += k
        return self.n


def build_nc(dbg=False, phase_limit=PHASE_LIMIT):
    nc = bass.Bass("TRN2", target_bir_lowering=False)
    sk = "ExternalOutput" if dbg else "Internal"

    x_own = nc.dram_tensor("x_own", [TOWN, D], F32, kind="ExternalInput")
    x_rest = nc.dram_tensor("x_rest", [BTOK - TOWN, D], F32, kind="ExternalInput")
    wqkv = nc.dram_tensor("wqkv", [D, FQ], F32, kind="ExternalInput")
    bqkv_pp = nc.dram_tensor("bqkv_pp", [128, 24], F32, kind="ExternalInput")
    bqkv_row = nc.dram_tensor("bqkv_row", [1, FQ], F32, kind="ExternalInput")
    wo = nc.dram_tensor("wo", [DH, D], F32, kind="ExternalInput")
    bo_row = nc.dram_tensor("bo_row", [1, D], F32, kind="ExternalInput")
    wgT = nc.dram_tensor("wgT", [D, INTER], I32, kind="ExternalInput")
    wuT = nc.dram_tensor("wuT", [D, INTER], I32, kind="ExternalInput")
    wdT = nc.dram_tensor("wdT", [INTER, D], I32, kind="ExternalInput")
    sg_pp = nc.dram_tensor("sg_pp", [128, NIC], F32, kind="ExternalInput")
    su_pp = nc.dram_tensor("su_pp", [128, NIC], F32, kind="ExternalInput")
    sd_row = nc.dram_tensor("sd_row", [1, D], F32, kind="ExternalInput")
    ones_in = nc.dram_tensor("ones_in", [1, 128], BF16, kind="ExternalInput")
    ident_in = nc.dram_tensor("ident_in", [128, 128], BF16, kind="ExternalInput")

    out = nc.dram_tensor("out", [TOWN, D], F32, kind="ExternalOutput")

    xn_sc = nc.dram_tensor("xn_sc", [BTOK, D], BF16, kind=sk)
    kT_sc = nc.dram_tensor("kT_sc", [DH, BTOK], BF16, kind=sk)
    v_sc = nc.dram_tensor("v_sc", [BTOK, DH], BF16, kind=sk)
    qT_sc = nc.dram_tensor("qT_sc", [DH, TOWN], BF16, kind=sk)
    pa_sc = nc.dram_tensor("pa_sc", [TOWN, D], F32, kind=sk)
    h_sc = nc.dram_tensor("h_sc", [INTER, TOWN], BF16, kind=sk)

    ctx = ExitStack()
    with ctx:
        bqkv_sb = ctx.enter_context(nc.sbuf_tensor("bqkv_sb", [128, 24], F32))
        bq_s = ctx.enter_context(nc.sbuf_tensor("bq_s", [128, 8], F32))
        ones_bf = ctx.enter_context(nc.sbuf_tensor("ones_bf", [1, 128], BF16))
        ident_bf = ctx.enter_context(nc.sbuf_tensor("ident_bf", [128, 128], BF16))
        sg_sb = ctx.enter_context(nc.sbuf_tensor("sg_sb", [128, NIC], F32))
        su_sb = ctx.enter_context(nc.sbuf_tensor("su_sb", [128, NIC], F32))

        pre = Sem(nc, ctx, "pre")
        with nc.Block() as blk:
            @blk.sync
            def _(sync):
                pre.inc(sync.dma_start(out=bqkv_sb[:], in_=bqkv_pp[:, :]), 16)
                pre.inc(sync.dma_start(out=sg_sb[:], in_=sg_pp[:, :]), 16)
                pre.inc(sync.dma_start(out=su_sb[:], in_=su_pp[:, :]), 16)
                pre.inc(sync.dma_start(out=ones_bf[:], in_=ones_in[:, :]), 16)
                pre.inc(sync.dma_start(out=ident_bf[:], in_=ident_in[:, :]), 16)

            @blk.vector
            def _(vector):
                vector.wait_ge(pre.h, 80)
                vector.tensor_scalar_mul(bq_s[:], bqkv_sb[:, 0:8], float(ASCL))

        if phase_limit >= 1:
            with ExitStack() as pctx:
                xt = [pctx.enter_context(nc.sbuf_tensor(f"xt{i}", [128, D], F32)) for i in range(2)]
                xnb = [pctx.enter_context(nc.sbuf_tensor(f"xnb{i}", [128, D], BF16)) for i in range(2)]
                dump = pctx.enter_context(nc.sbuf_tensor("dump", [128, D], F32))
                st1 = [pctx.enter_context(nc.sbuf_tensor(f"st1_{i}", [128, 1], F32)) for i in range(2)]
                st2 = [pctx.enter_context(nc.sbuf_tensor(f"st2_{i}", [128, 1], F32)) for i in range(2)]
                mean = [pctx.enter_context(nc.sbuf_tensor(f"mean{i}", [128, 1], F32)) for i in range(2)]
                msq = [pctx.enter_context(nc.sbuf_tensor(f"msq{i}", [128, 1], F32)) for i in range(2)]
                var = [pctx.enter_context(nc.sbuf_tensor(f"var{i}", [128, 1], F32)) for i in range(2)]
                v4 = [pctx.enter_context(nc.sbuf_tensor(f"v4_{i}", [128, 1], F32)) for i in range(2)]
                lnv = [pctx.enter_context(nc.sbuf_tensor(f"lnv{i}", [128, 1], F32)) for i in range(2)]
                rstd = [pctx.enter_context(nc.sbuf_tensor(f"rstd{i}", [128, 1], F32)) for i in range(2)]
                lx = [Sem(nc, pctx, f"lx{i}") for i in range(2)]
                xnw = [Sem(nc, pctx, f"xnw{i}") for i in range(2)]
                vap = Sem(nc, pctx, "vap")
                vred = Sem(nc, pctx, "vred")
                ssq = Sem(nc, pctx, "ssq")
                svar = Sem(nc, pctx, "svar")
                srst = Sem(nc, pctx, "srst")
                sself = Sem(nc, pctx, "sself")

                def xsrc(g):
                    if g < 16:
                        return x_own[g * 128:(g + 1) * 128, :]
                    return x_rest[(g - 16) * 128:(g - 15) * 128, :]

                with nc.Block() as blk:
                    @blk.sync
                    def _(sync):
                        for g in range(NT):
                            if g % 2 == 0:
                                if g >= 2:
                                    sync.wait_ge(vap.h, g - 1)
                                    sync.wait_ge(ssq.h, g - 1)
                                sync.dma_start(out=xt[0][:], in_=xsrc(g)).then_inc(lx[0].h, 16)
                                lx[0].n += 16
                            if g >= 1:
                                sync.wait_ge(vap.h, g)
                                pb = (g - 1) % 2
                                sync.dma_start(
                                    out=xn_sc[(g - 1) * 128:g * 128, :], in_=xnb[pb][:]
                                ).then_inc(xnw[pb].h, 16)
                                xnw[pb].n += 16
                        sync.wait_ge(vap.h, NT)
                        sync.dma_start(out=xn_sc[(NT - 1) * 128:NT * 128, :], in_=xnb[(NT - 1) % 2][:]
                                       ).then_inc(xnw[(NT - 1) % 2].h, 16)
                        xnw[(NT - 1) % 2].n += 16
                        sync.wait_ge(xnw[0].h, xnw[0].n)
                        sync.wait_ge(xnw[1].h, xnw[1].n)

                    @blk.gpsimd
                    def _(gpsimd):
                        for g in range(1, NT, 2):
                            if g >= 2:
                                gpsimd.wait_ge(vap.h, g - 1)
                                gpsimd.wait_ge(ssq.h, g - 1)
                            gpsimd.dma_start(out=xt[1][:], in_=xsrc(g)).then_inc(lx[1].h, 16)
                            lx[1].n += 16

                    @blk.vector
                    def _(vector):
                        for g in range(NT):
                            b = g % 2
                            vector.wait_ge(lx[b].h, 16 * (g // 2 + 1))
                            if g >= 2:
                                vector.wait_ge(xnw[b].h, 16 * (g // 2))
                            vred.inc(vector.reduce_sum(st1[b][:], xt[b][:], axis=AX.X), 1)
                            vector.wait_ge(vred.h, vred.n)
                            vred.inc(vector.tensor_single_scalar(mean[b][:], st1[b][:], float(1.0 / D), ALU.mult), 1)
                            vector.wait_ge(vred.h, vred.n)
                            vector.tensor_mul(msq[b][:], mean[b][:], mean[b][:])
                            vector.wait_ge(ssq.h, g + 1)
                            vred.inc(vector.tensor_single_scalar(v4[b][:], st2[b][:], float(1.0 / D), ALU.mult), 1)
                            vector.wait_ge(vred.h, vred.n)
                            vector.tensor_sub(var[b][:], v4[b][:], msq[b][:]).then_inc(svar.h, 1)
                            vector.wait_ge(srst.h, g + 1)
                            vector.tensor_scalar(
                                xnb[b][:], xt[b][:], mean[b][:, 0:1], rstd[b][:, 0:1],
                                op0=ALU.subtract, op1=ALU.mult,
                            ).then_inc(vap.h, 1)

                    @blk.scalar
                    def _(scalar):
                        for g in range(NT):
                            b = g % 2
                            scalar.wait_ge(lx[b].h, 16 * (g // 2 + 1))
                            scalar.activation(dump[:], xt[b][:], AF.Square,
                                              accum_out=st2[b][:, 0:1]).then_inc(ssq.h, 1)
                            scalar.wait_ge(svar.h, g + 1)
                            sself.inc(scalar.activation(lnv[b][:], var[b][:], AF.Ln, bias=np.float32(EPS)), 1)
                            scalar.wait_ge(sself.h, sself.n)
                            scalar.activation(rstd[b][:], lnv[b][:], AF.Exp,
                                              scale=np.float32(-0.5)).then_inc(srst.h, 1)

        if phase_limit >= 2:
            with ExitStack() as pctx:
                bvrow_bf = pctx.enter_context(nc.sbuf_tensor("bvrow_bf", [1, DH], BF16))
                wk_sb = pctx.enter_context(nc.sbuf_tensor("wk_sb", [128, CC * 1024], BF16))
                wv_sb = pctx.enter_context(nc.sbuf_tensor("wv_sb", [128, CC * 1024], BF16))
                xnt = [pctx.enter_context(nc.sbuf_tensor(f"xnt{i}", [128, CC * TB], BF16)) for i in range(2)]
                ktile = [pctx.enter_context(nc.sbuf_tensor(f"ktile{i}", [128, TB], BF16)) for i in range(4)]
                vtile = [pctx.enter_context(nc.sbuf_tensor(f"vtile{i}", [128, DH], BF16)) for i in range(4)]
                psK = [pctx.enter_context(nc.psum_tensor(f"psK{i}", [128, 512], F32)) for i in range(2)]
                psV = [pctx.enter_context(nc.psum_tensor(f"psV{i}", [128, DH], F32)) for i in range(2)]
                wkv_ld = Sem(nc, pctx, "wkv_ld")
                xl = [Sem(nc, pctx, f"xl{i}") for i in range(2)]
                kmm = Sem(nc, pctx, "kmm")
                vmm = Sem(nc, pctx, "vmm")
                kev = Sem(nc, pctx, "kev")
                vev = Sem(nc, pctx, "vev")
                kst = [Sem(nc, pctx, f"kst{i}") for i in range(4)]
                vst = [Sem(nc, pctx, f"vst{i}") for i in range(4)]

                with nc.Block() as blk:
                    @blk.gpsimd
                    def _(gpsimd):
                        wkv_ld.inc(gpsimd.dma_start(out=bvrow_bf[:], in_=bqkv_row[0:1, 2048:3072]), 16)
                        for cc in range(CC):
                            wkv_ld.inc(gpsimd.dma_start(
                                out=wk_sb[:, cc * 1024:(cc + 1) * 1024],
                                in_=wqkv[cc * 128:(cc + 1) * 128, 1024:2048]), 16)
                        for cc in range(CC):
                            wkv_ld.inc(gpsimd.dma_start(
                                out=wv_sb[:, cc * 1024:(cc + 1) * 1024],
                                in_=wqkv[cc * 128:(cc + 1) * 128, 2048:3072]), 16)

                    @blk.sync
                    def _(sync):
                        for tb in range(NTB + 1):
                            if tb < NTB:
                                b = tb % 2
                                if tb >= 2:
                                    sync.wait_ge(vmm.h, 4 * (tb - 1))
                                for cc in range(CC):
                                    sync.dma_start_transpose(
                                        xnt[b][:, cc * TB:(cc + 1) * TB],
                                        xn_sc[tb * TB:(tb + 1) * TB, cc * 128:(cc + 1) * 128],
                                    ).then_inc(xl[b].h, 16)
                                    xl[b].n += 16
                            if tb >= 1:
                                ptb = tb - 1
                                for dc in range(NDC):
                                    kg = ptb * NDC + dc
                                    sync.wait_ge(kev.h, kg + 1)
                                    sync.dma_start(
                                        out=kT_sc[dc * 128:(dc + 1) * 128, ptb * TB:(ptb + 1) * TB],
                                        in_=ktile[kg % 4][:],
                                    ).then_inc(kst[kg % 4].h, 16)
                                    kst[kg % 4].n += 16
                                for ts in range(4):
                                    vg = ptb * 4 + ts
                                    sync.wait_ge(vev.h, vg + 1)
                                    sync.dma_start(
                                        out=v_sc[ptb * TB + ts * 128:ptb * TB + (ts + 1) * 128, :],
                                        in_=vtile[vg % 4][:],
                                    ).then_inc(vst[vg % 4].h, 16)
                                    vst[vg % 4].n += 16
                        for i in range(4):
                            sync.wait_ge(kst[i].h, kst[i].n)
                            sync.wait_ge(vst[i].h, vst[i].n)

                    @blk.tensor
                    def _(tensor):
                        tensor.wait_ge(wkv_ld.h, 16 * 65)
                        for tb in range(NTB):
                            b = tb % 2
                            tensor.wait_ge(xl[b].h, 16 * CC * (tb // 2 + 1))
                            for dc in range(NDC):
                                kg = tb * NDC + dc
                                if kg >= 2:
                                    tensor.wait_ge(kev.h, kg - 1)
                                for cc in range(CC):
                                    mm = tensor.matmul(
                                        psK[kg % 2][:],
                                        wk_sb[:, cc * 1024 + dc * 128:cc * 1024 + (dc + 1) * 128],
                                        xnt[b][:, cc * TB:(cc + 1) * TB],
                                        start=(cc == 0), stop=(cc == CC - 1),
                                    )
                                kmm.inc(mm, 1)
                            for ts in range(4):
                                vg = tb * 4 + ts
                                if vg >= 2:
                                    tensor.wait_ge(vev.h, vg - 1)
                                for cc in range(CC):
                                    lhs = xnt[b][:, cc * TB + ts * 128:cc * TB + (ts + 1) * 128]
                                    tensor.matmul(psV[vg % 2][:, 0:512], lhs,
                                                  wv_sb[:, cc * 1024:cc * 1024 + 512],
                                                  start=(cc == 0), stop=False)
                                    tensor.matmul(psV[vg % 2][:, 512:1024], lhs,
                                                  wv_sb[:, cc * 1024 + 512:cc * 1024 + 1024],
                                                  start=(cc == 0), stop=False)
                                tensor.matmul(psV[vg % 2][:, 0:512], ones_bf[0:1, :],
                                              bvrow_bf[0:1, 0:512], start=False, stop=True)
                                mm = tensor.matmul(psV[vg % 2][:, 512:1024], ones_bf[0:1, :],
                                                   bvrow_bf[0:1, 512:1024], start=False, stop=True)
                                vmm.inc(mm, 1)

                    @blk.scalar
                    def _(scalar):
                        for kg in range(NTB * NDC):
                            dc = kg % NDC
                            scalar.wait_ge(kmm.h, kg + 1)
                            if kg >= 4:
                                scalar.wait_ge(kst[kg % 4].h, 16 * (kg // 4))
                            kev.inc(scalar.activation(
                                ktile[kg % 4][:], psK[kg % 2][:], AF.Identity,
                                bias=bqkv_sb[:, 8 + dc:9 + dc]), 1)

                    @blk.vector
                    def _(vector):
                        for vg in range(NTB * 4):
                            vector.wait_ge(vmm.h, vg + 1)
                            if vg >= 4:
                                vector.wait_ge(vst[vg % 4].h, 16 * (vg // 4))
                            vev.inc(vector.tensor_copy(vtile[vg % 4][:], psV[vg % 2][:]), 1)

        if phase_limit >= 3:
            with ExitStack() as pctx:
                wq_sb = pctx.enter_context(nc.sbuf_tensor("wq_sb", [128, CC * 1024], BF16))
                xnt3 = [pctx.enter_context(nc.sbuf_tensor(f"xnt3_{i}", [128, CC * TB], BF16)) for i in range(2)]
                qtile = [pctx.enter_context(nc.sbuf_tensor(f"qtile{i}", [128, TB], BF16)) for i in range(4)]
                psQ = [pctx.enter_context(nc.psum_tensor(f"psQ{i}", [128, 512], F32)) for i in range(2)]
                wq_ld = Sem(nc, pctx, "wq_ld")
                x3 = [Sem(nc, pctx, f"x3_{i}") for i in range(2)]
                qmm = Sem(nc, pctx, "qmm")
                qev = Sem(nc, pctx, "qev")
                qst = [Sem(nc, pctx, f"qst{i}") for i in range(4)]

                with nc.Block() as blk:
                    @blk.gpsimd
                    def _(gpsimd):
                        for cc in range(CC):
                            wq_ld.inc(gpsimd.dma_start(
                                out=wq_sb[:, cc * 1024:(cc + 1) * 1024],
                                in_=wqkv[cc * 128:(cc + 1) * 128, 0:1024]), 16)

                    @blk.sync
                    def _(sync):
                        for tb in range(NTB_OWN + 1):
                            if tb < NTB_OWN:
                                b = tb % 2
                                if tb >= 2:
                                    sync.wait_ge(qmm.h, NDC * (tb - 1))
                                for cc in range(CC):
                                    sync.dma_start_transpose(
                                        xnt3[b][:, cc * TB:(cc + 1) * TB],
                                        xn_sc[tb * TB:(tb + 1) * TB, cc * 128:(cc + 1) * 128],
                                    ).then_inc(x3[b].h, 16)
                                    x3[b].n += 16
                            if tb >= 1:
                                ptb = tb - 1
                                for dc in range(NDC):
                                    qg = ptb * NDC + dc
                                    sync.wait_ge(qev.h, qg + 1)
                                    sync.dma_start(
                                        out=qT_sc[dc * 128:(dc + 1) * 128, ptb * TB:(ptb + 1) * TB],
                                        in_=qtile[qg % 4][:],
                                    ).then_inc(qst[qg % 4].h, 16)
                                    qst[qg % 4].n += 16
                        for i in range(4):
                            sync.wait_ge(qst[i].h, qst[i].n)

                    @blk.tensor
                    def _(tensor):
                        tensor.wait_ge(wq_ld.h, 16 * CC)
                        for tb in range(NTB_OWN):
                            b = tb % 2
                            tensor.wait_ge(x3[b].h, 16 * CC * (tb // 2 + 1))
                            for dc in range(NDC):
                                qg = tb * NDC + dc
                                if qg >= 2:
                                    tensor.wait_ge(qev.h, qg - 1)
                                for cc in range(CC):
                                    mm = tensor.matmul(
                                        psQ[qg % 2][:],
                                        wq_sb[:, cc * 1024 + dc * 128:cc * 1024 + (dc + 1) * 128],
                                        xnt3[b][:, cc * TB:(cc + 1) * TB],
                                        start=(cc == 0), stop=(cc == CC - 1),
                                    )
                                qmm.inc(mm, 1)

                    @blk.scalar
                    def _(scalar):
                        for qg in range(NTB_OWN * NDC):
                            dc = qg % NDC
                            scalar.wait_ge(qmm.h, qg + 1)
                            if qg >= 4:
                                scalar.wait_ge(qst[qg % 4].h, 16 * (qg // 4))
                            qev.inc(scalar.activation(
                                qtile[qg % 4][:], psQ[qg % 2][:], AF.Identity,
                                bias=bq_s[:, dc:dc + 1], scale=np.float32(ASCL)), 1)

        if phase_limit >= 4:
            kT_r = kT_sc.ap().rearrange("(dc p) k -> p dc k", p=128)
            v_r = v_sc.ap().rearrange("(g p) d -> p g d", p=128)
            qT_r = qT_sc.ap().rearrange("(dc p) q -> p dc q", p=128)
            NG, QG, NCH, CHK = 2, 8, 8, 1024
            NS = NG * NCH * QG
            titP = {}
            titO = {}
            _t = 0
            for _g in range(NG):
                for _m in range(_g * 64, (_g + 1) * 64):
                    titP[_m] = _t
                    _t += 8
                for _q in range(_g * 8, (_g + 1) * 8):
                    titO[_q] = _t
                    _t += 8
            apr_cum = []
            _a = 0
            for _n in range(NS):
                if (_n // 8) % NCH > 0:
                    _a += 1
                apr_cum.append(_a)

            with ExitStack() as pctx:
                bo_bf = pctx.enter_context(nc.sbuf_tensor("bo_bf", [1, D], BF16))
                wo_sb = pctx.enter_context(nc.sbuf_tensor("wo_sb", [128, NDC * D], BF16))
                q_sb = pctx.enter_context(nc.sbuf_tensor("q_sb", [128, NDC * 1024], BF16))
                kt_sb = [pctx.enter_context(nc.sbuf_tensor(f"kt_sb{i}", [128, NDC * CHK], BF16)) for i in range(2)]
                v_sb = [pctx.enter_context(nc.sbuf_tensor(f"v_sb{i}", [128, 8 * DH], BF16)) for i in range(2)]
                P_sb = [pctx.enter_context(nc.sbuf_tensor(f"P_sb{i}", [128, CHK], BF16)) for i in range(2)]
                pt_all = pctx.enter_context(nc.sbuf_tensor("pt_all", [128, 8 * 128], BF16))
                pt_sb = [pt_all[:, i * 128:(i + 1) * 128] for i in range(8)]
                O_sb = pctx.enter_context(nc.sbuf_tensor("O_sb", [128, QG * DH], F32))
                obf = [pctx.enter_context(nc.sbuf_tensor(f"obf{i}", [128, DH], BF16)) for i in range(2)]
                ot_sb = [pctx.enter_context(nc.sbuf_tensor(f"ot_sb{i}", [128, DH], BF16)) for i in range(2)]
                stat = pctx.enter_context(nc.sbuf_tensor("stat", [128, 8 * QG], F32))
                negm_pp = stat[:, 0:2 * QG]
                mtmp = stat[:, 2 * QG:3 * QG]
                db = stat[:, 3 * QG:4 * QG]
                alb = stat[:, 4 * QG:5 * QG]
                lcb = stat[:, 5 * QG:6 * QG]
                lsum_run = stat[:, 6 * QG:7 * QG]
                rlb = stat[:, 7 * QG:8 * QG]
                ostg = [pctx.enter_context(nc.sbuf_tensor(f"ostg{i}", [128, 512], F32)) for i in range(2)]
                psS = [pctx.enter_context(nc.psum_tensor(f"psS{i}", [128, CHK], F32)) for i in range(2)]
                psO = pctx.enter_context(nc.psum_tensor("psO", [128, DH], F32))
                psPT = [pctx.enter_context(nc.psum_tensor(f"psPT{i}", [128, 1024], BF16)) for i in range(2)]

                def slotPR(pg):
                    return psS[(pg % 4) // 2][:, (pg % 2) * 512:(pg % 2 + 1) * 512]

                wol = Sem(nc, pctx, "wol")
                qld = Sem(nc, pctx, "qld")
                kld = [Sem(nc, pctx, f"kld{i}") for i in range(2)]
                vld = [Sem(nc, pctx, f"vld{i}") for i in range(2)]
                smm = Sem(nc, pctx, "smm")
                dvp = Sem(nc, pctx, "dvp")
                aprep = Sem(nc, pctx, "aprep")
                pexp = Sem(nc, pctx, "pexp")
                ptm = Sem(nc, pctx, "ptm")
                ptc = Sem(nc, pctx, "ptc")
                pvm = Sem(nc, pctx, "pvm")
                oacc = Sem(nc, pctx, "oacc")
                oev = Sem(nc, pctx, "oev")
                pmm = Sem(nc, pctx, "pmm")
                pev = Sem(nc, pctx, "pev")
                past = [Sem(nc, pctx, f"past{i}") for i in range(2)]
                vsf = Sem(nc, pctx, "vsf")

                with nc.Block() as blk:
                    @blk.gpsimd
                    def _(gpsimd):
                        wol.inc(gpsimd.dma_start(out=bo_bf[:], in_=bo_row[:, :]), 16)
                        for lc in range(NG * NCH):
                            c = lc % NCH
                            if lc >= 2:
                                gpsimd.wait_ge(pvm.h, (lc - 1) * 64)
                            vld[lc % 2].inc(gpsimd.dma_start(
                                out=v_sb[lc % 2].ap().rearrange("p (g d) -> p g d", d=DH),
                                in_=v_r[:, c * 8:(c + 1) * 8, :]), 16)
                            if 2 <= lc < 2 + NDC:
                                dc = lc - 2
                                wol.inc(gpsimd.dma_start(
                                    out=wo_sb[:, dc * D:(dc + 1) * D],
                                    in_=wo[dc * 128:(dc + 1) * 128, :]), 16)

                    @blk.sync
                    def _(sync):
                        def kt_load(lc):
                            c = lc % NCH
                            if lc >= 2:
                                sync.wait_ge(smm.h, (lc - 1) * 8)
                            kld[lc % 2].inc(sync.dma_start(
                                out=kt_sb[lc % 2].ap().rearrange("p (dc k) -> p dc k", k=CHK),
                                in_=kT_r[:, :, c * CHK:(c + 1) * CHK]), 16)

                        def tail_sync(g):
                            for qt in range(QG):
                                q_i = g * 8 + qt
                                for ob in range(NOB):
                                    pg = q_i * 8 + ob
                                    sync.wait_ge(pev.h, pg + 1)
                                    past[pg % 2].inc(sync.dma_start(
                                        out=pa_sc[q_i * 128:(q_i + 1) * 128,
                                                  ob * 512:(ob + 1) * 512],
                                        in_=ostg[pg % 2][:]), 16)

                        qld.inc(sync.dma_start(
                            out=q_sb.ap().rearrange("p (dc q) -> p dc q", q=1024),
                            in_=qT_r[:, :, 0:1024]), 16)
                        for lc in range(0, 8):
                            kt_load(lc)
                        sync.wait_ge(smm.h, 64)
                        qld.inc(sync.dma_start(
                            out=q_sb.ap().rearrange("p (dc q) -> p dc q", q=1024),
                            in_=qT_r[:, :, 1024:2048]), 16)
                        kt_load(8)
                        kt_load(9)
                        tail_sync(0)
                        for lc in range(10, 16):
                            kt_load(lc)
                        tail_sync(1)
                        sync.wait_ge(past[0].h, past[0].n)
                        sync.wait_ge(past[1].h, past[1].n)

                    @blk.tensor
                    def _(tensor):
                        def pv_block(n_p):
                            lc_p = n_p // 8
                            pb = n_p % 2
                            t0 = titP[n_p]
                            tensor.wait_ge(vld[lc_p % 2].h, 16 * (lc_p // 2 + 1))
                            tensor.wait_ge(oacc.h, n_p)
                            tensor.wait_ge(pexp.h, n_p + 1)
                            for kc in range(9):
                                if kc < 8:
                                    t = t0 + kc
                                    if t >= 2:
                                        tensor.wait_ge(ptc.h, t - 1)
                                    ptm.inc(tensor.transpose(
                                        psPT[t % 2][:, 0:128],
                                        P_sb[pb][:, kc * 128:(kc + 1) * 128],
                                        ident_bf[:]), 1)
                                if kc >= 1:
                                    pc = kc - 1
                                    tensor.wait_ge(ptc.h, t0 + pc + 1)
                                    tensor.matmul(psO[:, 0:512], pt_sb[pc % 8][:],
                                                  v_sb[lc_p % 2][:, pc * DH:pc * DH + 512],
                                                  start=(pc == 0), stop=(pc == 7))
                                    mm = tensor.matmul(psO[:, 512:1024], pt_sb[pc % 8][:],
                                                       v_sb[lc_p % 2][:, pc * DH + 512:(pc + 1) * DH],
                                                       start=(pc == 0), stop=(pc == 7))
                                    pvm.inc(mm, 1)

                        for g in range(NG):
                            for c in range(NCH):
                                lc = g * NCH + c
                                tensor.wait_ge(kld[lc % 2].h, 16 * (lc // 2 + 1))
                                if c == 0:
                                    tensor.wait_ge(qld.h, 16 * (g + 1))
                                for qt in range(QG):
                                    n_s = lc * 8 + qt
                                    b = n_s % 2
                                    if n_s >= 2:
                                        tensor.wait_ge(pexp.h, n_s - 1)
                                    if g == 1 and c == 0 and qt < 2:
                                        tensor.wait_ge(pev.h, 62 if qt == 0 else 64)
                                    for dc in range(NDC):
                                        q_lhsT = q_sb[:, dc * 1024 + qt * 128:dc * 1024 + (qt + 1) * 128]
                                        tensor.matmul(
                                            psS[b][:, 0:512], q_lhsT,
                                            kt_sb[lc % 2][:, dc * CHK:dc * CHK + 512],
                                            start=(dc == 0), stop=(dc == NDC - 1))
                                        mm = tensor.matmul(
                                            psS[b][:, 512:1024], q_lhsT,
                                            kt_sb[lc % 2][:, dc * CHK + 512:(dc + 1) * CHK],
                                            start=(dc == 0), stop=(dc == NDC - 1))
                                    smm.inc(mm, 1)
                                    if n_s - 1 >= g * 64:
                                        pv_block(n_s - 1)
                            pv_block((g + 1) * 64 - 1)
                            if g == 0:
                                tensor.wait_ge(wol.h, 16 * (NDC + 1))
                            for qt in range(QG):
                                q_i = g * 8 + qt
                                tensor.wait_ge(oev.h, q_i + 1)
                                t0 = titO[q_i]
                                for dc in range(NDC):
                                    t = t0 + dc
                                    tensor.wait_ge(ptc.h, t - 1)
                                    ptm.inc(tensor.transpose(
                                        psPT[t % 2][:, 0:128],
                                        obf[qt % 2][:, dc * 128:(dc + 1) * 128],
                                        ident_bf[:]), 1)
                                tensor.wait_ge(ptc.h, t0 + 8)
                                for ob in range(NOB):
                                    pg = q_i * 8 + ob
                                    if pg >= 4:
                                        tensor.wait_ge(pev.h, pg - 3)
                                    for dc in range(NDC):
                                        tensor.matmul(
                                            slotPR(pg),
                                            ot_sb[qt % 2][:, dc * 128:(dc + 1) * 128],
                                            wo_sb[:, dc * D + ob * 512:dc * D + (ob + 1) * 512],
                                            start=(dc == 0), stop=False)
                                    mm = tensor.matmul(slotPR(pg), ones_bf[0:1, :],
                                                       bo_bf[0:1, ob * 512:(ob + 1) * 512],
                                                       start=False, stop=True)
                                    pmm.inc(mm, 1)

                    @blk.scalar
                    def _(scalar):
                        def pt_copies(m):
                            t0 = titP[m]
                            for kc in range(8):
                                t = t0 + kc
                                scalar.wait_ge(ptm.h, t + 1)
                                if m >= 1:
                                    scalar.wait_ge(pvm.h, (m - 1) * 8 + kc + 1)
                                ptc.inc(scalar.copy(pt_sb[kc][:], psPT[t % 2][:, 0:128]), 1)

                        for g in range(NG):
                            for c in range(NCH):
                                lc = g * NCH + c
                                for qt in range(QG):
                                    n_s = lc * 8 + qt
                                    if n_s - 1 >= g * 64:
                                        pt_copies(n_s - 1)
                                    scalar.wait_ge(dvp.h, n_s + 1)
                                    if n_s >= 2:
                                        scalar.wait_ge(ptm.h, titP[n_s - 2] + 8)
                                    if c > 0:
                                        aprep.inc(scalar.activation(
                                            alb[:, qt:qt + 1], db[:, qt:qt + 1], AF.Exp), 1)
                                    acc = lcb[:, qt:qt + 1] if c > 0 else lsum_run[:, qt:qt + 1]
                                    pexp.inc(scalar.activation(
                                        P_sb[n_s % 2][:], psS[n_s % 2][:], AF.Exp,
                                        bias=negm_pp[:, 2 * qt + (c % 2):2 * qt + (c % 2) + 1],
                                        accum_out=acc), 1)
                            pt_copies((g + 1) * 64 - 1)
                            for qt in range(QG):
                                q_i = g * 8 + qt
                                t0 = titO[q_i]
                                for dc in range(NDC):
                                    t = t0 + dc
                                    scalar.wait_ge(ptm.h, t + 1)
                                    if q_i >= 2:
                                        scalar.wait_ge(pmm.h, (q_i - 1) * 8)
                                    ptc.inc(scalar.copy(
                                        ot_sb[qt % 2][:, dc * 128:(dc + 1) * 128],
                                        psPT[t % 2][:, 0:128]), 1)

                    @blk.vector
                    def _(vector):
                        vsf_n = [0]

                        def vself(instr):
                            vsf_n[0] += 1
                            instr.then_inc(vsf.h, 1)
                            vector.wait_ge(vsf.h, vsf_n[0])

                        for g in range(NG):
                            for c in range(NCH):
                                lc = g * NCH + c
                                for qt in range(QG):
                                    n_s = lc * 8 + qt
                                    b = n_s % 2
                                    vector.wait_ge(smm.h, n_s + 1)
                                    cn = 2 * qt + c % 2
                                    co = 2 * qt + (c + 1) % 2
                                    if c == 0:
                                        dvp.inc(vector.reduce_max(
                                            negm_pp[:, cn:cn + 1], psS[b][:],
                                            axis=AX.X, negate=True), 1)
                                    else:
                                        vself(vector.reduce_max(
                                            mtmp[:, qt:qt + 1], psS[b][:],
                                            axis=AX.X, negate=True))
                                        vself(vector.tensor_tensor(
                                            negm_pp[:, cn:cn + 1], negm_pp[:, co:co + 1],
                                            mtmp[:, qt:qt + 1], op=ALU.min))
                                        dvp.inc(vector.tensor_sub(
                                            db[:, qt:qt + 1], negm_pp[:, cn:cn + 1],
                                            negm_pp[:, co:co + 1]), 1)
                                    if c > 0:
                                        vector.wait_ge(pexp.h, n_s + 1)
                                        vector.wait_ge(aprep.h, apr_cum[n_s])
                                        lupd = vector.tensor_scalar(
                                            lsum_run[:, qt:qt + 1], lsum_run[:, qt:qt + 1],
                                            alb[:, qt:qt + 1], lcb[:, qt:qt + 1],
                                            op0=ALU.mult, op1=ALU.add)
                                        if c == NCH - 1:
                                            vsf_n[0] += 1
                                            lupd.then_inc(vsf.h, 1)
                                    vector.wait_ge(pvm.h, (n_s + 1) * 8)
                                    ov = O_sb[:, qt * DH:(qt + 1) * DH]
                                    if c == 0:
                                        oacc.inc(vector.tensor_copy(ov, psO[:]), 1)
                                    else:
                                        oacc.inc(vector.scalar_tensor_tensor(
                                            ov, ov, alb[:, qt:qt + 1], psO[:],
                                            op0=ALU.mult, op1=ALU.add), 1)
                            vector.wait_ge(vsf.h, vsf_n[0])

                            def evac_qt(qt):
                                q_i = g * 8 + qt
                                for ob in range(NOB):
                                    pg = q_i * 8 + ob
                                    vector.wait_ge(pmm.h, pg + 1)
                                    if pg >= 2:
                                        vector.wait_ge(past[pg % 2].h, 16 * (pg // 2))
                                    pev.inc(vector.tensor_copy(
                                        ostg[pg % 2][:], slotPR(pg)), 1)

                            for qt in range(QG):
                                q_i = g * 8 + qt
                                vself(vector.reciprocal(rlb[:, qt:qt + 1],
                                                        lsum_run[:, qt:qt + 1]))
                                if q_i >= 2:
                                    vector.wait_ge(ptm.h, titO[q_i - 2] + 8)
                                oev.inc(vector.tensor_scalar_mul(
                                    obf[qt % 2][:], O_sb[:, qt * DH:(qt + 1) * DH],
                                    rlb[:, qt:qt + 1]), 1)
                                if qt >= 1:
                                    evac_qt(qt - 1)
                            evac_qt(QG - 1)

        if phase_limit >= 5:
            wgT_r = wgT.ap().rearrange("(cc p) i -> p cc i", p=128)
            wuT_r = wuT.ap().rearrange("(cc p) i -> p cc i", p=128)
            with ExitStack() as pctx:
                xno = pctx.enter_context(nc.sbuf_tensor("xno", [128, CC * TOWN], BF16))
                wg_sb = [pctx.enter_context(nc.sbuf_tensor(f"wg_sb{i}", [128, CC * 128], BF16)) for i in range(2)]
                wu_sb = [pctx.enter_context(nc.sbuf_tensor(f"wu_sb{i}", [128, CC * 128], BF16)) for i in range(2)]
                silu_sb = [pctx.enter_context(nc.sbuf_tensor(f"silu{i}", [128, 512], BF16)) for i in range(2)]
                h_sb = [pctx.enter_context(nc.sbuf_tensor(f"h_sb{i}", [128, TOWN], BF16)) for i in range(2)]
                psG = [pctx.enter_context(nc.psum_tensor(f"psG{i}", [128, 512], F32)) for i in range(4)]
                psU = [pctx.enter_context(nc.psum_tensor(f"psU{i}", [128, 512], F32)) for i in range(4)]
                xnol = Sem(nc, pctx, "xnol")
                wl = [Sem(nc, pctx, f"wl{i}") for i in range(2)]
                gmm = Sem(nc, pctx, "gmm")
                sil = Sem(nc, pctx, "sil")
                hev = Sem(nc, pctx, "hev")
                hst = [Sem(nc, pctx, f"hst{i}") for i in range(2)]

                with nc.Block() as blk:
                    @blk.sync
                    def _(sync):
                        for cc in range(CC):
                            sync.dma_start_transpose(
                                xno[:, cc * TOWN:(cc + 1) * TOWN],
                                xn_sc[0:TOWN, cc * 128:(cc + 1) * 128],
                            ).then_inc(xnol.h, 16)
                            xnol.n += 16
                        for ic in range(NIC):
                            sync.wait_ge(hev.h, (ic + 1) * NTBF)
                            sync.dma_start(out=h_sc[ic * 128:(ic + 1) * 128, :],
                                           in_=h_sb[ic % 2][:]).then_inc(hst[ic % 2].h, 16)
                            hst[ic % 2].n += 16
                        for i in range(2):
                            sync.wait_ge(hst[i].h, hst[i].n)

                    @blk.gpsimd
                    def _(gpsimd):
                        ccr = [(0, 8), (8, 16), (16, 24), (24, 32)]
                        for ic in range(NIC):
                            b = ic % 2
                            if ic >= 2:
                                gpsimd.wait_ge(gmm.h, NTBF * (ic - 1))
                            for (c0, c1) in ccr:
                                wl[b].inc(gpsimd.dma_start(
                                    out=wg_sb[b].ap().rearrange("p (cc i) -> p cc i", i=128)[:, c0:c1, :],
                                    in_=wgT_r[:, c0:c1, ic * 128:(ic + 1) * 128]), 16)
                                wl[b].inc(gpsimd.dma_start(
                                    out=wu_sb[b].ap().rearrange("p (cc i) -> p cc i", i=128)[:, c0:c1, :],
                                    in_=wuT_r[:, c0:c1, ic * 128:(ic + 1) * 128]), 16)

                    @blk.tensor
                    def _(tensor):
                        tensor.wait_ge(xnol.h, 16 * CC)
                        for ic in range(NIC):
                            b = ic % 2
                            tensor.wait_ge(wl[b].h, 16 * 8 * (ic // 2 + 1))
                            for tf in range(NTBF):
                                gg = ic * NTBF + tf
                                if gg >= 4:
                                    tensor.wait_ge(hev.h, gg - 3)
                                for cc in range(CC):
                                    rhs = xno[:, cc * TOWN + tf * 512:cc * TOWN + (tf + 1) * 512]
                                    tensor.matmul(psG[tf][:],
                                                  wg_sb[b][:, cc * 128:(cc + 1) * 128], rhs,
                                                  start=(cc == 0), stop=(cc == CC - 1))
                                    mm = tensor.matmul(psU[tf][:],
                                                       wu_sb[b][:, cc * 128:(cc + 1) * 128], rhs,
                                                       start=(cc == 0), stop=(cc == CC - 1))
                                gmm.inc(mm, 1)

                    @blk.scalar
                    def _(scalar):
                        for ic in range(NIC):
                            for tf in range(NTBF):
                                gg = ic * NTBF + tf
                                scalar.wait_ge(gmm.h, gg + 1)
                                if gg >= 2:
                                    scalar.wait_ge(hev.h, gg - 1)
                                sil.inc(scalar.activation(
                                    silu_sb[gg % 2][:], psG[tf][:], AF.Silu,
                                    scale=sg_sb[:, ic:ic + 1]), 1)

                    @blk.vector
                    def _(vector):
                        for ic in range(NIC):
                            for tf in range(NTBF):
                                gg = ic * NTBF + tf
                                vector.wait_ge(sil.h, gg + 1)
                                if ic >= 2 and tf == 0:
                                    vector.wait_ge(hst[ic % 2].h, 16 * (ic // 2))
                                hev.inc(vector.scalar_tensor_tensor(
                                    h_sb[ic % 2][:, tf * 512:(tf + 1) * 512],
                                    psU[tf][:], su_sb[:, ic:ic + 1], silu_sb[gg % 2][:],
                                    op0=ALU.mult, op1=ALU.mult), 1)

        if phase_limit >= 6:
            h_r = h_sc.ap().rearrange("(ic p) t -> p ic t", p=128)
            with ExitStack() as pctx:
                h_res = pctx.enter_context(nc.sbuf_tensor("h_res", [128, NIC * TG], BF16))
                wd_sb = [pctx.enter_context(nc.sbuf_tensor(f"wd_sb{i}", [128, 512], BF16)) for i in range(6)]
                sd_sb = [pctx.enter_context(nc.sbuf_tensor(f"sd_sb{i}", [128, 512], F32)) for i in range(2)]
                pa_sb = [pctx.enter_context(nc.sbuf_tensor(f"pa_sb{i}", [128, 512], F32)) for i in range(3)]
                xt_sb = [pctx.enter_context(nc.sbuf_tensor(f"xt_sb{i}", [128, 512], F32)) for i in range(3)]
                tmp_sb = [pctx.enter_context(nc.sbuf_tensor(f"tmp_sb{i}", [128, 512], F32)) for i in range(2)]
                out_sb = [pctx.enter_context(nc.sbuf_tensor(f"out_sb{i}", [128, 512], F32)) for i in range(3)]
                psD = pctx.enter_context(nc.psum_tensor("psD", [128, NTS * 512], F32))
                hrl = Sem(nc, pctx, "hrl")
                wdl6 = [Sem(nc, pctx, f"wdl6_{i}") for i in range(6)]
                sdl = [Sem(nc, pctx, f"sdl{i}") for i in range(2)]
                pal = [Sem(nc, pctx, f"pal{i}") for i in range(3)]
                xtl = [Sem(nc, pctx, f"xtl{i}") for i in range(3)]
                dic = Sem(nc, pctx, "dic")
                dmm = Sem(nc, pctx, "dmm")
                dev = Sem(nc, pctx, "dev")
                outst = [Sem(nc, pctx, f"outst{i}") for i in range(3)]
                vs6 = Sem(nc, pctx, "vs6")

                ICR = [(0, 11), (11, 22), (22, 33), (33, 44), (44, 55), (55, 66), (66, 76), (76, 86)]

                with nc.Block() as blk:
                    @blk.sync
                    def _(sync):
                        for tg in range(NTG):
                            for (i0, i1) in ICR:
                                if tg >= 1:
                                    sync.wait_ge(dic.h, (tg * NOB - 1) * NIC + i1)
                                sync.dma_start(
                                    out=h_res.ap().rearrange("p (ic t) -> p ic t", t=TG)[:, i0:i1, :],
                                    in_=h_r[:, i0:i1, tg * TG:(tg + 1) * TG],
                                ).then_inc(hrl.h, 16)
                                hrl.n += 16


                    @blk.scalar
                    def _(scalar):
                        def out_store(k):
                            ktg, krem = k // (NOB * NTS), k % (NOB * NTS)
                            kob, kts = krem // NTS, krem % NTS
                            scalar.dma_start(
                                out=out[ktg * TG + kts * 128:ktg * TG + (kts + 1) * 128,
                                        kob * 512:(kob + 1) * 512],
                                in_=out_sb[k % 3][:],
                            ).then_inc(outst[k % 3].h, 16)
                            outst[k % 3].n += 16

                        for tg in range(NTG):
                            for ob in range(NOB):
                                tgob = tg * NOB + ob
                                if tgob >= 2:
                                    scalar.wait_ge(dev.h, (tgob - 1) * NTS)
                                scalar.dma_start(
                                    out=sd_sb[ob % 2][:],
                                    in_=sd_row[0:1, ob * 512:(ob + 1) * 512].broadcast_to([128, 512]),
                                ).then_inc(sdl[ob % 2].h, 16)
                                sdl[ob % 2].n += 16
                                for ts in range(NTS):
                                    pai = tgob * NTS + ts
                                    if pai >= 3:
                                        scalar.wait_ge(dev.h, pai - 2)
                                        out_store(pai - 3)
                                    scalar.dma_start(
                                        out=pa_sb[pai % 3][:],
                                        in_=pa_sc[tg * TG + ts * 128:tg * TG + (ts + 1) * 128,
                                                  ob * 512:(ob + 1) * 512],
                                    ).then_inc(pal[pai % 3].h, 16)
                                    pal[pai % 3].n += 16
                                    scalar.dma_start(
                                        out=xt_sb[pai % 3][:],
                                        in_=x_own[tg * TG + ts * 128:tg * TG + (ts + 1) * 128,
                                                  ob * 512:(ob + 1) * 512],
                                    ).then_inc(xtl[pai % 3].h, 16)
                                    xtl[pai % 3].n += 16
                        npai = NTG * NOB * NTS
                        for k in range(npai - 3, npai):
                            scalar.wait_ge(dev.h, k + 1)
                            out_store(k)
                        for i in range(3):
                            scalar.wait_ge(outst[i].h, outst[i].n)

                    @blk.gpsimd
                    def _(gpsimd):
                        for tg in range(NTG):
                            for ob in range(NOB):
                                for ic in range(NIC):
                                    gl = (tg * NOB + ob) * NIC + ic
                                    if gl >= 6:
                                        gpsimd.wait_ge(dic.h, gl - 5)
                                    wdl6[gl % 6].inc(gpsimd.dma_start(
                                        out=wd_sb[gl % 6][:],
                                        in_=wdT[ic * 128:(ic + 1) * 128,
                                                ob * 512:(ob + 1) * 512]), 16)

                    @blk.tensor
                    def _(tensor):
                        for tg in range(NTG):
                            tensor.wait_ge(hrl.h, 16 * 8 * (tg + 1))
                            for ob in range(NOB):
                                tgob = tg * NOB + ob
                                for ic in range(NIC):
                                    gl = tgob * NIC + ic
                                    b = gl % 6
                                    tensor.wait_ge(wdl6[gl % 6].h, 16 * (gl // 6 + 1))
                                    for ts in range(NTS):
                                        if tgob >= 1 and ic == 0:
                                            tensor.wait_ge(dev.h, (tgob - 1) * NTS + ts + 1)
                                        mm = tensor.matmul(
                                            psD[:, ts * 512:(ts + 1) * 512],
                                            h_res[:, ic * TG + ts * 128:ic * TG + (ts + 1) * 128],
                                            wd_sb[b][:],
                                            start=(ic == 0), stop=(ic == NIC - 1),
                                        )
                                        if ts == NTS - 1:
                                            dic.inc(mm, 1)
                                        elif ic == NIC - 1:
                                            dmm.inc(mm, 1)

                    @blk.vector
                    def _(vector):
                        for tg in range(NTG):
                            for ob in range(NOB):
                                tgob = tg * NOB + ob
                                vector.wait_ge(sdl[ob % 2].h, 16 * (tgob // 2 + 1))
                                for ts in range(NTS):
                                    pai = tgob * NTS + ts
                                    if ts < NTS - 1:
                                        vector.wait_ge(dmm.h, tgob * (NTS - 1) + ts + 1)
                                    else:
                                        vector.wait_ge(dic.h, (tgob + 1) * NIC)
                                    vector.wait_ge(pal[pai % 3].h, 16 * (pai // 3 + 1))
                                    vector.wait_ge(xtl[pai % 3].h, 16 * (pai // 3 + 1))
                                    if pai >= 3:
                                        vector.wait_ge(outst[pai % 3].h, 16 * (pai // 3))
                                    vs6.inc(vector.tensor_mul(tmp_sb[pai % 2][:],
                                                      psD[:, ts * 512:(ts + 1) * 512],
                                                      sd_sb[ob % 2][:]), 1)
                                    vector.wait_ge(vs6.h, vs6.n)
                                    vs6.inc(vector.tensor_add(tmp_sb[pai % 2][:],
                                                              tmp_sb[pai % 2][:],
                                                              pa_sb[pai % 3][:]), 1)
                                    vector.wait_ge(vs6.h, vs6.n)
                                    dev.inc(vector.tensor_add(out_sb[pai % 3][:],
                                                              tmp_sb[pai % 2][:],
                                                              xt_sb[pai % 3][:]), 1)

    return nc


BF = ml_dtypes.bfloat16


def kernel(L0_attn_x, L0_attn_Wqkv, L0_attn_bqkv, L0_attn_Wo, L0_attn_bo,
           L0_ffn_w_gate, L0_ffn_s_gate, L0_ffn_w_up, L0_ffn_s_up,
           L0_ffn_w_down, L0_ffn_s_down, _core_ids=None, _dbg=False,
           _phase_limit=PHASE_LIMIT):
    x = np.ascontiguousarray(np.asarray(L0_attn_x, dtype=np.float32))
    wqkv = np.ascontiguousarray(np.asarray(L0_attn_Wqkv, dtype=np.float32))
    bqkv = np.asarray(L0_attn_bqkv, dtype=np.float32)
    wo = np.ascontiguousarray(np.asarray(L0_attn_Wo, dtype=np.float32))
    bo = np.asarray(L0_attn_bo, dtype=np.float32)
    wgT = np.ascontiguousarray(np.asarray(L0_ffn_w_gate, dtype=np.int32).T)
    wuT = np.ascontiguousarray(np.asarray(L0_ffn_w_up, dtype=np.int32).T)
    wdT = np.ascontiguousarray(np.asarray(L0_ffn_w_down, dtype=np.int32).T)
    sg = np.asarray(L0_ffn_s_gate, dtype=np.float32)
    su = np.asarray(L0_ffn_s_up, dtype=np.float32)
    sd = np.asarray(L0_ffn_s_down, dtype=np.float32)

    bqkv_pp = np.ascontiguousarray(bqkv.reshape(24, 128).T)
    bqkv_row = np.ascontiguousarray(bqkv.reshape(1, FQ))
    bo_row = np.ascontiguousarray(bo.reshape(1, D))
    sg_pp = np.ascontiguousarray(sg.reshape(NIC, 128).T)
    su_pp = np.ascontiguousarray(su.reshape(NIC, 128).T)
    sd_row = np.ascontiguousarray(sd.reshape(1, D))
    ones_in = np.ones((1, 128), dtype=BF)
    ident_in = np.eye(128, dtype=BF)

    core_ids = list(range(8)) if _core_ids is None else _core_ids
    in_maps = []
    for c in core_ids:
        b, r = c // 4, c % 4
        xb = x[b]
        own = np.ascontiguousarray(xb[r * TOWN:(r + 1) * TOWN])
        rest = np.ascontiguousarray(np.concatenate([xb[:r * TOWN], xb[(r + 1) * TOWN:]], axis=0))
        in_maps.append({
            "x_own": own, "x_rest": rest, "wqkv": wqkv, "bqkv_pp": bqkv_pp,
            "bqkv_row": bqkv_row, "wo": wo, "bo_row": bo_row,
            "wgT": wgT, "wuT": wuT, "wdT": wdT,
            "sg_pp": sg_pp, "su_pp": su_pp, "sd_row": sd_row,
            "ones_in": ones_in, "ident_in": ident_in,
        })

    nc = build_nc(dbg=_dbg, phase_limit=_phase_limit)
    res = run_bass_kernel_spmd(nc, in_maps, core_ids=list(range(len(core_ids))))
    if _dbg:
        return res
    shards = [res.results[i]["out"] for i in range(len(core_ids))]
    full = np.concatenate(shards, axis=0)
    return full.reshape(2, BTOK, D)
